# version 07
# baseline (speedup 1.0000x reference)
import contextlib
import numpy as np
import ml_dtypes
import concourse.bass as bass
import concourse.mybir as mybir
from concourse.bass_utils import run_bass_kernel_spmd

F32 = mybir.dt.float32
BF16 = mybir.dt.bfloat16
I32 = mybir.dt.int32
AF = mybir.ActivationFunctionType
ALU = mybir.AluOpType
AXX = mybir.AxisListType.X

NCORES = 8
NSEQ = 2
D = 1024
T = 2048
H = 16
F4 = 4096
NT = 4
TW = 512
NC8 = 8
EPS = 1e-5

_SR = {}
def _mk_sr():
    off = 0
    for name, ln in [("bout", D), ("gamma", D), ("beta", D), ("fc1b", F4),
                     ("fc2b", D), ("binz", D), ("binB", 16), ("binC", 16),
                     ("binD", 16), ("alog", 16), ("bout1", D)]:
        if ln >= 128:
            off = ((off + 127) // 128) * 128
        _SR[name] = (off, ln)
        off += ln
    return ((off + 2047) // 2048) * 2048
SR_LEN = _mk_sr()


class Prog:
    ENGS = ("sync", "gpsimd", "scalar", "vector", "tensor")

    def __init__(self):
        self.items = {e: [] for e in self.ENGS}

    def raw(self, eng, fn):
        self.items[eng].append(fn)

    def op(self, eng, method, *args, inc=None, inck=1, waits=(), **kw):
        waits = [(c.h, v) for (c, v) in waits]
        inch = None
        if inc is not None:
            inc.n += inck
            inch = inc.h

        def fn(e, waits=waits, method=method, args=args, kw=kw, inch=inch,
               inck=inck):
            for h, v in waits:
                e.wait_ge(h, v)
            instr = getattr(e, method)(*args, **kw)
            if inch is not None:
                instr.then_inc(inch, inck)
            return instr
        self.items[eng].append(fn)

    def wait(self, eng, ctr, val=None):
        v = ctr.n if val is None else val
        self.items[eng].append(lambda e, h=ctr.h, v=v: e.wait_ge(h, v))


class Ctr:
    def __init__(self, stk, nc, name):
        self.h = stk.enter_context(nc.semaphore(name))
        self.n = 0


def build(phases=("L0", "L1"), x_in_name="xT", out_name="outT", taps=(), lvl=99):
    nc = bass.Bass("TRN2", target_bir_lowering=False, debug=False)
    stk = contextlib.ExitStack()
    taps = set(taps)
    do_l0 = "L0" in phases
    do_l1 = "L1" in phases
    pr = Prog()

    def din(name, shape, dt=F32):
        return nc.dram_tensor(name, shape, dt, kind="ExternalInput").ap()

    x_h = din(x_in_name, [NSEQ, D, T])
    out_h = nc.dram_tensor(out_name, [NSEQ, D, T], F32, kind="ExternalOutput").ap()
    sr_h = din("smallrow", [1, SR_LEN])
    i128f_h = din("i128f", [128, 128])
    band_h = din("bandbf", [128, 896], BF16)
    if do_l0:
        wqkvT_h = din("wqkvT", [D, 3 * D])
        woutT_h = din("woutT", [D, D])
        fc1q_h = din("fc1q", [D, F4], mybir.dt.uint8)
        fc2q_h = din("fc2q", [F4, D], mybir.dt.uint8)
        fc1s_h = din("fc1s", [8, F4])
        fc2s_h = din("fc2s", [32, D])
        w1s = nc.dram_tensor("w1s", [D, F4], BF16, kind="Internal").ap()
        w2s = nc.dram_tensor("w2s", [F4, D], BF16, kind="Internal").ap()
    if do_l1:
        win_h = din("win", [D, 2096])
        wout1_h = din("wout1", [D, D])
        l1w_h = din("l1w", [32 * D, D])
        l1bf_h = din("l1bf", [1, 32 * D])
        ids_h = din("ids", [8], I32)
        wts_h = din("wts", [1, 8])
        wsel = nc.dram_tensor("wsel", [8 * D, D], F32, kind="Internal").ap()

    tap_hs = {}

    s_ld = Ctr(stk, nc, "s_ld")
    s_gp = Ctr(stk, nc, "s_gp")
    s_mm = Ctr(stk, nc, "s_mm")
    s_sc = Ctr(stk, nc, "s_sc")
    s_ve = Ctr(stk, nc, "s_ve")
    s_st = Ctr(stk, nc, "s_st")
    s_l = Ctr(stk, nc, "s_l")
    ALLC = [s_ld, s_gp, s_mm, s_sc, s_ve]

    def barrier():
        for e in Prog.ENGS:
            for c in ALLC:
                pr.wait(e, c)

    def tap_store(name, ap):
        if name not in taps or name in tap_hs:
            return
        barrier()
        tap_hs[name] = nc.dram_tensor("tap_" + name, list(ap.shape), ap.dtype,
                                      kind="ExternalOutput").ap()
        pr.op("sync", "dma_start", inc=s_st, inck=16,
              out=tap_hs[name][:, :], in_=ap)
        barrier()

    def tap_dram(name, dap):
        if name not in taps or name in tap_hs:
            return
        barrier()
        tap_hs[name] = nc.dram_tensor("tap_" + name, list(dap.shape), dap.dtype,
                                      kind="ExternalOutput").ap()
        pr.op("gpsimd", "dma_start", inc=s_gp, inck=16,
              out=tap_hs[name][:, :], in_=dap)
        barrier()

    sbP = lambda n, sh, dt: stk.enter_context(nc.sbuf_tensor(n, sh, dt))
    cols = sbP("cols", [128, 80], F32)
    COL = {"bout": 0, "gamma": 8, "beta": 16, "fc1b": 24, "fc2b5": 56,
           "binz": 64, "badd": 72}
    i128f = sbP("i128f_t", [128, 128], F32)
    band_t = sbP("band_t", [128, 896], BF16)
    onesc_bf = sbP("onesc_bf", [128, 1], BF16)
    onesr_f = sbP("onesr_f", [1, 128], F32)
    onesr3 = sbP("onesr3", [65, 128], F32)
    halfr = sbP("halfr", [65, 128], F32)
    onesr_b3 = sbP("onesr_b3", [65, 128], BF16)
    sca = sbP("sca", [1, 16], F32)
    epsc = sbP("epsc", [1, 2], F32)
    c0neg = sbP("c0neg", [128, 1], F32)
    ecol = sbP("ecol", [128, 1], F32)
    sc16 = sbP("sc16", [16, 8], F32)
    if do_l1:
        wbar_sb = sbP("wbar_sb", [128, 8 * D], BF16)
        wtsc_sb = sbP("wtsc", [8, 1], F32)
        wrep_sb = sbP("wrep", [128, 8], F32)
        ids_sb = sbP("ids_sb", [1, 8], I32)

    S_ps = stk.enter_context(nc.psum_tensor("S_ps", [128, T], F32))
    O_ps = stk.enter_context(nc.psum_tensor("O_ps", [128, T], F32))
    SB = lambda b: S_ps[:, b * 512:(b + 1) * 512]
    OB = lambda b: O_ps[:, b * 512:(b + 1) * 512]

    x_sb = stk.enter_context(nc.sbuf_tensor("x_sb", [128, NC8 * T], F32))
    rowsA = stk.enter_context(nc.sbuf_tensor("rowsA", [128, T], F32))
    rowsB = stk.enter_context(nc.sbuf_tensor("rowsB", [128, T], F32))
    pexp_sb = stk.enter_context(nc.sbuf_tensor("pexp_sb", [128, T], BF16))
    stg = stk.enter_context(nc.sbuf_tensor("stg", [128, TW], F32))
    stg2 = stk.enter_context(nc.sbuf_tensor("stg2", [128, TW], F32))

    stkS = contextlib.ExitStack()
    sbS = lambda n, sh, dt: stkS.enter_context(nc.sbuf_tensor(n, sh, dt))
    bbrow = sbS("bbrow", [1, 1024], F32)
    if do_l1:
        bsel_sb = sbS("bsel_sb", [8, D], F32)
        wkI_sb = sbS("wkI_sb", [128, 8 * 128], F32)
        ringS = sbS("ringS", [128, 6 * D], F32)
    if do_l0:
        qint = sbS("qint", [128, 2 * 2048], mybir.dt.uint8)
        qbf = sbS("qbf", [128, 2 * 2048], BF16)
        wtmp = sbS("wtmp", [128, 2 * 2048], BF16)

    pr.op("sync", "dma_start", inc=s_ld, inck=16, out=i128f[:], in_=i128f_h[:])
    pr.op("sync", "dma_start", inc=s_ld, inck=16, out=band_t[:], in_=band_h[:])
    _SLOT = [("A", 0), ("A", 32), ("A", 64), ("B", 0), ("B", 32), ("B", 64)]

    def srow_ap(off, ln):
        sl = off // 2048
        kind, r = _SLOT[sl]
        tns = rowsA if kind == "A" else rowsB
        o = off % 2048
        return tns[r:r + 1, o:o + ln], i128f[r:r + 1, r:r + 1]
    for sl in range(6):
        kind, r = _SLOT[sl]
        tns = rowsA if kind == "A" else rowsB
        w = min(2048, SR_LEN - sl * 2048)
        pr.op("sync", "dma_start", inc=s_ld, inck=16,
              out=tns[r:r + 1, 0:w], in_=sr_h[0:1, sl * 2048: sl * 2048 + w])
    pr.op("sync", "dma_start", inc=s_ld, inck=16,
          out=x_sb[:].rearrange("p (c t) -> p c t", c=NC8),
          in_=x_h[0].rearrange("(c p) t -> p c t", p=128))
    xld0_done = s_ld.n
    pr.wait("vector", s_ld)
    pr.op("vector", "memset", onesc_bf[:], 1.0)
    pr.op("vector", "memset", onesr_f[:], 1.0)
    pr.op("vector", "memset", onesr3[:], 1.0)
    pr.op("vector", "memset", halfr[:], 0.5)
    pr.op("vector", "memset", onesr_b3[:], 1.0)
    if do_l1:
        pr.raw("vector", lambda e: e.memset(sc16[0:16, 5:6], 1.0))
    pr.op("vector", "memset", sca[:], 0.0)
    pr.op("vector", "memset", epsc[0:1, 0:1], EPS)
    pr.op("vector", "memset", epsc[0:1, 1:2], EPS * EPS, inc=s_ve)

    pr.wait("tensor", s_ld)
    pr.wait("tensor", s_ve)
    colspec = []
    if do_l0:
        colspec += [("bout", COL["bout"], 8), ("gamma", COL["gamma"], 8),
                    ("beta", COL["beta"], 8), ("fc1b", COL["fc1b"], 32),
                    ("fc2b", COL["fc2b5"], 8)]
    if do_l1:
        colspec += [("binz", COL["binz"], 8), ("bout1", COL["badd"], 8)]
    flat = [(n, b, c) for (n, b, ncols) in colspec for c in range(ncols)]
    for i0 in range(0, len(flat), 16):
        grp = flat[i0:i0 + 16]
        for j, (name, base, c) in enumerate(grp):
            off, _ = _SR[name]
            sap, idap = srow_ap(off + c * 128, 128)
            pr.op("tensor", "transpose", O_ps[0:128, j:j + 1],
                  sap, idap, inc=(s_mm if j == len(grp) - 1 else None))
        pr.wait("vector", s_mm)
        for j, (name, base, c) in enumerate(grp):
            sc = 0.5 if name == "fc2b" else 1.0
            pr.op("vector", "tensor_scalar", out=cols[:, base + c:base + c + 1],
                  in0=O_ps[0:128, j:j + 1], scalar1=sc, scalar2=None,
                  op0=ALU.mult, inc=(s_ve if j == len(grp) - 1 else None))
        pr.wait("tensor", s_ve)
    if do_l1:
        for j, name in enumerate(["binB", "binC", "binD", "alog"]):
            off, _ = _SR[name]
            sap, idap = srow_ap(off, 16)
            pr.op("tensor", "transpose", O_ps[0:16, j:j + 1],
                  sap, idap, inc=(s_mm if j == 3 else None))
        pr.wait("vector", s_mm)
        pr.op("vector", "tensor_copy", out=sc16[0:16, 0:4], in_=O_ps[0:16, 0:4],
              inc=s_ve)
        pr.wait("scalar", s_ve)
        pr.op("scalar", "activation", out=sc16[0:16, 4:5], in_=sc16[0:16, 3:4],
              func=AF.Exp, inc=s_sc)
        pr.wait("vector", s_sc)
        pr.op("vector", "tensor_scalar", out=sc16[0:16, 4:5],
              in0=sc16[0:16, 4:5], scalar1=-1.0, scalar2=None, op0=ALU.mult,
              inc=s_ve)
        pr.wait("tensor", s_ve)

    if do_l1:
        pr.op("gpsimd", "dma_start", inc=s_gp, inck=16, out=ids_sb[:],
              in_=ids_h[:])
        pr.wait("gpsimd", s_gp)
        pr.op("gpsimd", "tensor_scalar", out=ids_sb[:], in0=ids_sb[:],
              scalar1=D, scalar2=None, op0=ALU.mult)

        def emit_gather(e):
            _, vals = nc.values_load_multi_w_load_instructions(
                ids_sb[0:1, 0:8], engines=[mybir.EngineType.Pool],
                skip_runtime_bounds_check=True)
            for k in range(8):
                e.dma_start(out=wsel[k * D:(k + 1) * D, :],
                            in_=l1w_h[bass.ds(vals[k], D), :]).then_inc(s_gp.h, 16)
                e.dma_start(
                    out=bsel_sb[k:k + 1, :],
                    in_=l1bf_h[0:1, bass.ds(vals[k], D)]
                ).then_inc(s_gp.h, 16)
        gather_mark = {}
        g0 = s_gp.n
        for k in range(8):
            gather_mark[k] = g0 + 32 * (k + 1)
        pr.raw("gpsimd", emit_gather)
        s_gp.n += 16 * 16
        pr.op("gpsimd", "dma_start", inc=s_gp, inck=16, out=sca[0:1, 0:8],
              in_=wts_h[:], waits=[(s_ve, s_ve.n)])

        pr.wait("tensor", s_gp)
        pr.op("tensor", "transpose", O_ps[0:8, 16:17], sca[0:1, 0:8],
              i128f[0:1, 0:1])
        pr.op("tensor", "matmul", O_ps[0:128, 512:520], onesr_f[:],
              sca[0:1, 0:8], start=True, stop=True, inc=s_mm)
        pr.wait("vector", s_mm)
        pr.op("vector", "tensor_copy", out=wtsc_sb[:], in_=O_ps[0:8, 16:17])
        pr.op("vector", "tensor_copy", out=wrep_sb[:], in_=O_ps[0:128, 512:520],
              inc=s_ve)
        pr.wait("vector", s_ve)
        for k in range(8):
            pr.op("vector", "tensor_scalar", out=wkI_sb[:, k * 128:(k + 1) * 128],
                  in0=i128f[:], scalar1=wrep_sb[:, k:k + 1], scalar2=None,
                  op0=ALU.mult, inc=(s_ve if k == 7 else None))

        pr.wait("tensor", s_ve)
        pr.op("tensor", "matmul", O_ps[0:1, 1024:1536], wtsc_sb[:],
              bsel_sb[:, 0:512], start=True, stop=True)
        pr.op("tensor", "matmul", O_ps[0:1, 1536:2048], wtsc_sb[:],
              bsel_sb[:, 512:1024], start=True, stop=True, inc=s_mm)
        pr.wait("scalar", s_mm)
        pr.op("scalar", "copy", out=bbrow[0:1, 0:1024], in_=O_ps[0:1, 1024:2048],
              inc=s_sc)
        pr.wait("tensor", s_sc)
        for c in range(8):
            pr.op("tensor", "transpose", S_ps[0:128, c:c + 1],
                  bbrow[0:1, c * 128:(c + 1) * 128], i128f[0:1, 0:1],
                  inc=(s_mm if c == 7 else None))
        pr.wait("vector", s_mm)
        pr.op("vector", "tensor_tensor", out=cols[:, COL["badd"]:COL["badd"] + 8],
              in0=cols[:, COL["badd"]:COL["badd"] + 8], in1=S_ps[0:128, 0:8],
              op=ALU.add, inc=s_ve)

        barrier()
        RN = 6
        gp0, mm0, ve0 = s_gp.n, s_mm.n, s_ve.n
        for oc in range(8):
            for k in range(8):
                idx = oc * 8 + k
                slot = idx % RN
                w = [] if idx < RN else [(s_mm, mm0 + (idx - RN + 1))]
                w.append((s_gp, gather_mark[k]))
                pr.op("gpsimd", "dma_start", inc=s_gp, inck=16, waits=w,
                      out=ringS[:, slot * D:(slot + 1) * D],
                      in_=wsel[k * D + oc * 128: k * D + (oc + 1) * 128, :])
        for oc in range(8):
            for k in range(8):
                idx = oc * 8 + k
                slot = idx % RN
                w = [(s_gp, gp0 + 16 * (idx + 1))]
                if k == 0 and oc > 0:
                    w.append((s_ve, ve0 + oc))
                for dc in range(8):
                    pr.op("tensor", "matmul", S_ps[:, dc * 128:(dc + 1) * 128],
                          ringS[:, slot * D + dc * 128: slot * D + (dc + 1) * 128],
                          wkI_sb[:, k * 128:(k + 1) * 128],
                          start=(k == 0), stop=(k == 7),
                          waits=(w if dc == 0 else []),
                          inc=(s_mm if dc == 7 else None))
            pr.wait("vector", s_mm, mm0 + (oc + 1) * 8)
            for dc in range(8):
                pr.op("vector", "tensor_copy",
                      out=wbar_sb[:, dc * D + oc * 128: dc * D + (oc + 1) * 128],
                      in_=S_ps[:, dc * 128:(dc + 1) * 128],
                      inc=(s_ve if dc == 7 else None))

    if do_l0:
        jobs = ([("fc1", g, hh, 2048) for g in range(8) for hh in range(2)]
                + [("fc2", g, 0, 1024) for g in range(32)])
        ld0, gp0, sc0, mm0, ve0 = s_ld.n, s_gp.n, s_sc.n, s_mm.n, s_ve.n
        for j, (kind, g, hh, w) in enumerate(jobs):
            slot = j % 2
            wq = [] if j < 2 else [(s_sc, sc0 + (j - 1))]
            src = (fc1q_h[g * 128:(g + 1) * 128, hh * 2048:(hh + 1) * 2048]
                   if kind == "fc1" else fc2q_h[g * 128:(g + 1) * 128, :])
            pr.op("sync", "dma_start", inc=s_ld, inck=16, waits=wq,
                  out=qint[:, slot * 2048: slot * 2048 + w], in_=src)
            wr = [(s_mm, mm0)] if j == 0 else [(s_mm, mm0 + j)]
            srcs = (fc1s_h[g:g + 1, hh * 2048:(hh + 1) * 2048] if kind == "fc1"
                    else fc2s_h[g:g + 1, :])
            pr.op("gpsimd", "dma_start", inc=s_gp, inck=16, waits=wr,
                  out=rowsB[64:65, 0:w], in_=srcs)
            wc = [(s_ld, ld0 + 16 * (2 * j + 1))]
            if j >= 2:
                wc.append((s_ve, ve0 + (j - 1)))
            pr.op("scalar", "activation", out=qbf[:, slot * 2048: slot * 2048 + w],
                  in_=qint[:, slot * 2048: slot * 2048 + w], func=AF.Copy,
                  bias=-128.0, waits=wc, inc=s_sc)
            wt = [(s_gp, gp0 + 16 * (j + 1))]
            if j > 0:
                wt.append((s_ve, ve0 + j))
            else:
                wt.append((s_ve, ve0))
                wt.append((s_sc, sc0))
            repl = onesr3[64:65, :] if kind == "fc1" else halfr[64:65, :]
            for b in range(w // 512):
                pr.op("tensor", "matmul", OB(b), repl,
                      rowsB[64:65, b * 512:(b + 1) * 512], start=True, stop=True,
                      waits=(wt if b == 0 else []),
                      inc=(s_mm if b == w // 512 - 1 else None))
            wm = [(s_sc, sc0 + j + 1), (s_mm, mm0 + j + 1)]
            if j >= 2:
                wm.append((s_ld, ld0 + 16 * (2 * j - 2)))
            pr.op("vector", "tensor_tensor",
                  out=wtmp[:, slot * 2048: slot * 2048 + w],
                  in0=qbf[:, slot * 2048: slot * 2048 + w],
                  in1=O_ps[:, 0:w], op=ALU.mult, waits=wm, inc=s_ve)
            dst = (w1s[g * 128:(g + 1) * 128, hh * 2048:(hh + 1) * 2048]
                   if kind == "fc1" else w2s[g * 128:(g + 1) * 128, :])
            pr.op("sync", "dma_start", inc=s_ld, inck=16,
                  waits=[(s_ve, ve0 + j + 1)],
                  out=dst, in_=wtmp[:, slot * 2048: slot * 2048 + w])

    barrier()
    if do_l0:
        tap_dram("w1sd", w1s)
        tap_dram("w2sd", w2s)

    stkS.close()
    sbM = lambda n, sh, dt: stk.enter_context(nc.sbuf_tensor(n, sh, dt))
    xn_sb = sbM("xn_sb", [128, NC8 * T], BF16)
    yo_sb = sbM("yo_sb", [128, NC8 * T], BF16)
    ring = sbM("ring", [128, 3 * 2048], BF16)

    lall = rowsB[0:16, :]
    lrow = rowsB[64:65, :]
    RA = lambda r, sl: rowsA[r:r + 1, sl]
    RB = lambda r, sl: rowsB[r:r + 1, sl]
    xch = lambda c, t0=0, w=T: x_sb[:, c * T + t0: c * T + t0 + w]
    xnch = lambda c, t0=0, w=T: xn_sb[:, c * T + t0: c * T + t0 + w]
    SL = slice(0, T)

    def emit_stats(sqstage):
        sc0 = s_sc.n
        ve0 = s_ve.n
        mm0 = s_mm.n
        for c in range(NC8):
            wcast = [] if c == 0 else [(s_mm, mm0 + 2 * c)]
            pr.op("vector", "tensor_copy", out=pexp_sb[:], in_=xch(c),
                  waits=wcast, inc=s_ve)
            pr.op("scalar", "activation", out=sqstage[:], in_=xch(c),
                  func=AF.Square, waits=wcast, inc=s_sc)
            pr.wait("tensor", s_ve, ve0 + c + 1)
            for b in range(4):
                pr.op("tensor", "matmul", S_ps[0:1, b * 512:(b + 1) * 512],
                      onesc_bf[:], pexp_sb[:, b * 512:(b + 1) * 512],
                      start=(c == 0), stop=(c == NC8 - 1),
                      inc=(s_mm if b == 3 else None))
            pr.wait("tensor", s_sc, sc0 + c + 1)
            for b in range(4):
                pr.op("tensor", "matmul", O_ps[0:1, b * 512:(b + 1) * 512],
                      onesc_bf[:], sqstage[:, b * 512:(b + 1) * 512],
                      start=(c == 0), stop=(c == NC8 - 1),
                      inc=(s_mm if b == 3 else None))
        pr.wait("vector", s_mm)
        pr.op("vector", "tensor_scalar", out=RA(0, SL), in0=S_ps[0:1, :],
              scalar1=1.0 / D, scalar2=None, op0=ALU.mult)
        pr.op("vector", "tensor_scalar", out=RB(64, SL), in0=O_ps[0:1, :],
              scalar1=1.0 / D, scalar2=None, op0=ALU.mult, inc=s_ve)
        pr.wait("vector", s_ve)
        pr.op("vector", "tensor_tensor", out=RA(64, SL), in0=RA(0, SL),
              in1=RA(0, SL), op=ALU.mult, inc=s_ve)
        pr.wait("vector", s_ve)
        pr.op("vector", "tensor_tensor", out=RA(64, SL), in0=RB(64, SL),
              in1=RA(64, SL), op=ALU.subtract, inc=s_ve)
        pr.wait("scalar", s_ve)
        pr.op("scalar", "activation", out=RB(0, SL), in_=RA(64, SL), func=AF.Ln,
              bias=epsc[0:1, 0:1])
        pr.op("scalar", "activation", out=RA(32, SL), in_=RB(0, SL), func=AF.Exp,
              scale=-0.5, inc=s_sc)

    def emit_L0(s, qp_sb, kp_sb, vp_sb, h2_sb):
        if s > 0:
            pr.op("sync", "dma_start", inc=s_ld, inck=16,
                  waits=[(s_st, s_st.n)],
                  out=x_sb[:].rearrange("p (c t) -> p c t", c=NC8),
                  in_=x_h[s].rearrange("(c p) t -> p c t", p=128))
        pr.wait("scalar", s_ld)
        pr.wait("tensor", s_ld)
        pr.wait("vector", s_ld)
        if lvl < 1:
            return
        emit_stats(h2_sb[:, 0:T])
        pr.op("scalar", "activation", out=pexp_sb[0:1, :], in_=RA(64, SL),
              func=AF.Ln, scale=1.0 + EPS, bias=epsc[0:1, 1:2], inc=s_sc)
        pr.wait("vector", s_sc)
        pr.op("vector", "tensor_tensor", out=RB(0, SL), in0=RB(0, SL),
              in1=pexp_sb[0:1, :], op=ALU.subtract, inc=s_ve)
        pr.wait("scalar", s_ve)
        pr.op("scalar", "activation", out=RB(0, SL), in_=RB(0, SL), func=AF.Exp,
              scale=0.5, inc=s_sc)
        barrier()

        if lvl < 2:
            return
        prev_zbase = None
        prev_fc2_end = None
        prev_gadd = None
        for tt in range(NT):
            t0 = tt * TW
            tsl = slice(t0, t0 + TW)
            wr0 = [] if prev_zbase is None else [(s_ve, prev_zbase + 1)]
            wr1 = [] if prev_zbase is None else [(s_ve, prev_zbase + 2)]
            wr2 = [] if prev_zbase is None else [(s_ve, prev_zbase + 3)]
            pr.op("tensor", "matmul", SB(0), onesr_f[:], RA(0, tsl),
                  start=True, stop=True, waits=wr0)
            pr.op("tensor", "matmul", SB(1), onesr3[32:33, 0:128], RA(32, tsl),
                  start=True, stop=True, waits=wr1)
            pr.op("tensor", "matmul", SB(2), onesr_f[:], RB(0, tsl),
                  start=True, stop=True, waits=wr2, inc=s_mm)
            pr.wait("vector", s_mm)
            if prev_gadd is not None:
                pr.wait("vector", s_gp, prev_gadd)
            ve_h2base = s_ve.n
            for c in range(NC8):
                pr.op("vector", "tensor_tensor", out=stg[:], in0=xch(c, t0, TW),
                      in1=SB(0), op=ALU.subtract)
                pr.op("vector", "tensor_tensor", out=stg2[:], in0=stg[:],
                      in1=SB(1), op=ALU.mult)
                pr.op("vector", "tensor_copy", out=xnch(c, t0, TW), in_=stg2[:])
                pr.op("vector", "tensor_tensor", out=xch(c, t0, TW),
                      in0=xch(c, t0, TW), in1=stg2[:], op=ALU.add)
                pr.op("vector", "tensor_tensor", out=stg[:], in0=xnch(c, t0, TW),
                      in1=SB(2), op=ALU.mult)
                pr.op("vector", "tensor_scalar",
                      out=h2_sb[:, c * TW:(c + 1) * TW], in0=stg[:],
                      scalar1=cols[:, COL["gamma"] + c:COL["gamma"] + c + 1],
                      scalar2=cols[:, COL["beta"] + c:COL["beta"] + c + 1],
                      op0=ALU.mult, op1=ALU.add, inc=s_ve)
            ve_h2 = s_ve.n
            if tt == 1:
                tap_store("h2t0", h2_sb[:])
            ld0 = s_ld.n
            sc0f = s_sc.n
            for pld in range(16):
                if pld < 3:
                    w = [] if prev_fc2_end is None else [(s_mm, prev_fc2_end)]
                else:
                    w = [(s_sc, sc0f + 2 * (pld - 2))]
                pr.op("sync", "dma_start", inc=s_ld, inck=16, waits=w,
                      out=ring[:, (pld % 3) * 2048:(pld % 3) * 2048 + 2048]
                      .rearrange("p (h j) -> p h j", h=8),
                      in_=w1s[:, pld * 256:(pld + 1) * 256]
                      .rearrange("(h p) j -> p h j", p=128))
            for fc in range(32):
                pld = fc // 2
                slot = (pld % 3) * 2048 + (fc % 2) * 128
                bank = SB(3) if fc % 2 == 0 else OB(0)
                wts_ = [(s_ld, ld0 + 16 * (pld + 1))]
                if fc >= 2:
                    wts_.append((s_sc, sc0f + fc - 1))
                elif prev_zbase is not None:
                    wts_.append((s_ve, prev_zbase + (4 if fc == 0 else 5)))
                for hc in range(NC8):
                    w2_ = list(wts_) if hc == 0 else []
                    if fc < 2:
                        w2_.append((s_ve, ve_h2base + hc + 1))
                    pr.op("tensor", "matmul", bank,
                          ring[:, slot + hc * 256: slot + hc * 256 + 128],
                          h2_sb[:, hc * TW:(hc + 1) * TW],
                          start=(hc == 0), stop=(hc == 7),
                          waits=w2_,
                          inc=(s_mm if hc == 7 else None))
                pr.wait("scalar", s_mm)
                pr.op("scalar", "activation",
                      out=yo_sb[:, fc * TW:(fc + 1) * TW], in_=bank,
                      func=AF.Silu,
                      bias=cols[:, COL["fc1b"] + fc:COL["fc1b"] + fc + 1],
                      inc=s_sc)
            if tt == 0:
                tap_store("yt0", yo_sb[:])
            ld1 = s_ld.n
            mm1 = s_mm.n
            sc1 = s_sc.n
            for pld in range(16):
                w = [(s_sc, sc1)] if pld < 3 else [(s_mm, mm1 + 2 * (pld - 2))]
                pr.op("sync", "dma_start", inc=s_ld, inck=16, waits=w,
                      out=ring[:, (pld % 3) * 2048:(pld % 3) * 2048 + 2048]
                      .rearrange("p (q h) -> p q h", q=2),
                      in_=w2s[pld * 256:(pld + 1) * 256, :]
                      .rearrange("(q p) h -> p q h", p=128))
            for fc in range(32):
                pld = fc // 2
                slot = (pld % 3) * 2048 + (fc % 2) * 1024
                w = [(s_ld, ld1 + 16 * (pld + 1))]
                if fc == 0:
                    w.append((s_sc, sc1))
                for hc in range(NC8):
                    bank = SB(hc) if hc < 4 else OB(hc - 4)
                    pr.op("tensor", "matmul", bank,
                          ring[:, slot + hc * 128: slot + hc * 128 + 128],
                          yo_sb[:, fc * TW:(fc + 1) * TW],
                          start=(fc == 0), stop=(fc == 31),
                          waits=(w if hc == 0 else []),
                          inc=(s_mm if hc == 7 else None))
            prev_fc2_end = s_mm.n
            zb = s_ve.n
            gp_add0 = s_gp.n
            for hc in range(NC8):
                bank = SB(hc) if hc < 4 else OB(hc - 4)
                sslot = stg2 if hc % 2 == 0 else stg
                wv_ = [(s_mm, s_mm.n)]
                if hc >= 2:
                    wv_.append((s_gp, gp_add0 + hc - 1))
                pr.op("vector", "tensor_scalar", out=sslot[:], in0=bank,
                      scalar1=cols[:, COL["fc2b5"] + hc:COL["fc2b5"] + hc + 1],
                      scalar2=None, op0=ALU.add, waits=wv_, inc=s_ve)
                pr.op("gpsimd", "tensor_tensor", out=xch(hc, t0, TW),
                      in0=xch(hc, t0, TW), in1=sslot[:], op=ALU.add,
                      waits=[(s_ve, s_ve.n)], inc=s_gp)
            prev_zbase = zb
            prev_gadd = s_gp.n
        barrier()
        tap_store("xn", xn_sb[:])
        tap_store("cffn", x_sb[:])

        if lvl < 3:
            return
        pr.op("vector", "memset", vp_sb[:], 0.0, inc=s_ve)
        import os
        NHP = int(os.environ.get("K_NHP", "8"))
        NHI = int(os.environ.get("K_NHI", "2"))
        NKT = int(os.environ.get("K_NKT", "16"))
        KSUB = int(os.environ.get("K_SUB", "99"))
        gpmark = {}
        mm_pair_end = {}

        def qkv_loads(hp):
            pi = hp % 2
            gpmark[hp] = s_gp.n
            if hp < 2:
                w = [(s_mm, s_mm.n)]
            else:
                w = [(s_mm, mm_pair_end[hp - 2])]
            for j, colof in enumerate([hp * 128, D + hp * 128,
                                       2 * D + hp * 128]):
                pr.op("gpsimd", "dma_start", inc=s_gp, inck=16,
                      waits=(w if j == 0 else []),
                      out=ring[:, pi * 3072 + j * 1024: pi * 3072 + j * 1024 + 1024]
                      .rearrange("p (c j2) -> p c j2", c=8),
                      in_=wqkvT_h[:, colof:colof + 128]
                      .rearrange("(c p) j2 -> p c j2", p=128))
        qkv_loads(0)
        if NHP > 1:
            qkv_loads(1)
        for hp in range(NHP):
            pi = hp % 2
            rb = pi * 3072
            if KSUB < 2:
                continue
            pr.wait("tensor", s_gp, gpmark[hp] + 16)
            pr.wait("tensor", s_sc)
            for tt in range(NT):
                for c in range(NC8):
                    pr.op("tensor", "matmul", S_ps[:, tt * 512:(tt + 1) * 512],
                          ring[:, rb + c * 128: rb + (c + 1) * 128],
                          xnch(c, tt * TW, TW), start=(c == 0), stop=(c == 7),
                          inc=(s_mm if (tt == 3 and c == 7) else None))
            pr.wait("tensor", s_gp, gpmark[hp] + 32)
            for tt in range(NT):
                for c in range(NC8):
                    pr.op("tensor", "matmul", O_ps[:, tt * 512:(tt + 1) * 512],
                          ring[:, rb + 1024 + c * 128: rb + 1024 + c * 128 + 128],
                          xnch(c, tt * TW, TW), start=(c == 0), stop=(c == 7),
                          inc=(s_mm if (tt == 3 and c == 7) else None))
            pr.wait("scalar", s_mm, s_mm.n - 1)
            pr.op("scalar", "copy", out=qp_sb[:], in_=S_ps[:], inc=s_sc)
            pr.wait("scalar", s_mm)
            pr.op("scalar", "copy", out=kp_sb[:], in_=O_ps[:], inc=s_sc)
            if KSUB < 3:
                continue
            for j, src_sb in enumerate([qp_sb, kp_sb]):
                pr.wait("scalar", s_sc, s_sc.n - 1 + j)
                if j == 1:
                    pr.wait("scalar", s_ve)
                pr.op("scalar", "activation", out=pexp_sb[:], in_=src_sb[:],
                      func=AF.Square, inc=s_sc)
                pr.wait("tensor", s_sc)
                for b in range(4):
                    pr.op("tensor", "matmul", O_ps[0:1, b * 512:(b + 1) * 512],
                          onesc_bf[:], pexp_sb[:, b * 512:(b + 1) * 512],
                          start=True, stop=True,
                          inc=(s_mm if b == 3 else None))
                pr.wait("vector", s_mm)
                pr.op("vector", "tensor_reduce", out=sca[0:1, j:j + 1],
                      in_=O_ps[0:1, :], axis=AXX, op=ALU.max, inc=s_ve)
            pr.wait("vector", s_ve)
            pr.op("vector", "tensor_tensor", out=sca[0:1, 2:3], in0=sca[0:1, 0:1],
                  in1=sca[0:1, 1:2], op=ALU.add, inc=s_ve)
            pr.wait("vector", s_ve)
            pr.op("vector", "tensor_scalar", out=sca[0:1, 2:3], in0=sca[0:1, 2:3],
                  scalar1=1.0 / 16, scalar2=None, op0=ALU.mult, inc=s_ve)
            pr.wait("vector", s_ve)
            pr.op("vector", "tensor_scalar", out=sca[0:1, 3:4], in0=sca[0:1, 2:3],
                  scalar1=-1.0, scalar2=None, op0=ALU.mult, inc=s_ve)
            pr.wait("scalar", s_ve)
            pr.op("scalar", "activation", out=sca[0:1, 4:5], in_=sca[0:1, 2:3],
                  func=AF.Exp, inc=s_sc)
            if KSUB < 4:
                continue
            pr.wait("tensor", s_sc)
            cc = 1024 + s * 32 + hp * 4
            pr.op("tensor", "matmul", S_ps[0:128, cc:cc + 1], onesr_f[:],
                  sca[0:1, 3:4], start=True, stop=True)
            pr.op("tensor", "matmul", S_ps[0:128, cc + 1:cc + 2], onesr_f[:],
                  sca[0:1, 4:5], start=True, stop=True, inc=s_mm)
            pr.wait("vector", s_mm)
            pr.op("vector", "tensor_copy", out=c0neg[:],
                  in_=S_ps[0:128, cc:cc + 1])
            pr.op("vector", "tensor_copy", out=ecol[:],
                  in_=S_ps[0:128, cc + 1:cc + 2], inc=s_ve)
            pr.wait("vector", s_ve)

            if KSUB < 5:
                continue
            vev0 = s_ve.n
            for Q in range(4):
                bank = SB(0) if Q % 2 == 0 else SB(1)
                w = [(s_ve, vev0)] if Q < 2 else [(s_ve, vev0 + (Q - 1) * 4)]
                for c in range(NC8):
                    for i in range(4):
                        tc = 4 * Q + i
                        pr.op("tensor", "matmul",
                              bank[0:128, i * 128:(i + 1) * 128],
                              xn_sb[:, c * T + tc * 128: c * T + (tc + 1) * 128],
                              ring[:, rb + 2048 + c * 128: rb + 2048 + c * 128 + 128],
                              start=(c == 0), stop=(c == 7),
                              waits=(w if (c == 0 and i == 0) else []),
                              inc=(s_mm if (c == 7 and i == 3) else None))
                pr.wait("vector", s_mm)
                for i in range(4):
                    tc = 4 * Q + i
                    base = tc * 144
                    pr.op("vector", "tensor_scalar",
                          out=vp_sb[:, base:base + 64],
                          in0=bank[0:128, i * 128:i * 128 + 64],
                          scalar1=ecol[:], scalar2=None, op0=ALU.mult)
                    pr.op("vector", "tensor_scalar",
                          out=vp_sb[:, base + 72:base + 136],
                          in0=bank[0:128, i * 128 + 64:i * 128 + 128],
                          scalar1=ecol[:], scalar2=None, op0=ALU.mult)
                    pr.op("vector", "tensor_scalar",
                          out=vp_sb[:, base + 64:base + 65], in0=onesc_bf[:],
                          scalar1=ecol[:], scalar2=None, op0=ALU.mult)
                    pr.op("vector", "tensor_scalar",
                          out=vp_sb[:, base + 136:base + 137], in0=onesc_bf[:],
                          scalar1=ecol[:], scalar2=None, op0=ALU.mult,
                          inc=s_ve)
            mm_pair_end[hp] = s_mm.n
            if hp + 2 < NHP:
                qkv_loads(hp + 2)

            prev_exp_last = None
            for hi in range(NHI):
                qs = slice(64 * hi, 64 * hi + 64)
                h = 2 * hp + hi
                pbuf = [pexp_sb, h2_sb]
                exp_mark = {}
                av_mark = {}
                mask_mark = {}
                ve_ent = s_ve.n
                sl_ent = s_l.n
                sc_ent = s_sc.n
                for kt in range(NKT):
                    qd = kt // 4
                    pb = pbuf[kt % 2]
                    for qtile in range(qd, 4):
                        wv = []
                        if (kt - 1, qtile) in exp_mark:
                            wv.append((s_sc, exp_mark[(kt - 1, qtile)]))
                        elif kt == 0:
                            if prev_exp_last is not None:
                                wv.append((s_sc, prev_exp_last[qtile]))
                            else:
                                wv.append((s_sc, sc_ent))
                            if hi == 0:
                                wv.append((s_ve, ve_ent))
                        pr.op("tensor", "matmul",
                              S_ps[:, qtile * 512:(qtile + 1) * 512],
                              kp_sb[qs, kt * 128:(kt + 1) * 128],
                              qp_sb[qs, qtile * 512:(qtile + 1) * 512],
                              start=True, stop=True, waits=wv, inc=s_mm)
                        mmv = s_mm.n
                        wexp = [(s_mm, mmv)]
                        if (kt - 2, qtile) in av_mark:
                            wexp.append((s_mm, av_mark[(kt - 2, qtile)]))
                        if kt == 0:
                            wexp += [(s_ve, ve_ent), (s_l, sl_ent)]
                        pr.op("scalar", "activation",
                              out=pb[:, qtile * 512:(qtile + 1) * 512],
                              in_=S_ps[:, qtile * 512:(qtile + 1) * 512],
                              func=AF.Exp, scale=0.125,
                              bias=c0neg[:], waits=wexp, inc=s_sc)
                        exp_mark[(kt, qtile)] = s_sc.n
                    r = kt * 128 - qd * 512
                    pr.op("vector", "tensor_tensor",
                          out=pb[:, qd * 512:qd * 512 + 512],
                          in0=pb[:, qd * 512:qd * 512 + 512],
                          in1=band_t[:, 384 - r:384 - r + 512], op=ALU.mult,
                          waits=[(s_sc, exp_mark[(kt, qd)])], inc=s_ve)
                    mask_mark[kt] = s_ve.n
                    for qtile in range(qd, 4):
                        if qtile == qd:
                            wav = [(s_ve, mask_mark[kt])]
                        else:
                            wav = [(s_sc, exp_mark[(kt, qtile)])]
                        pr.op("tensor", "matmul",
                              O_ps[0:65, qtile * 512:(qtile + 1) * 512],
                              vp_sb[:, kt * 144 + 72 * hi:
                                    kt * 144 + 72 * hi + 65],
                              pb[:, qtile * 512:(qtile + 1) * 512],
                              start=(kt == 0), stop=(kt == 4 * qtile + 3),
                              waits=wav, inc=s_mm)
                        av_mark[(kt, qtile)] = s_mm.n
                if True:
                    prev_exp_last = {q: exp_mark[(4 * q + 3, q)] for q in range(4)}
                pr.wait("scalar", s_mm)
                pr.op("scalar", "copy", out=pexp_sb[0:65, :], in_=O_ps[0:65, :],
                      inc=s_sc)
                pr.wait("gpsimd", s_sc)
                pr.op("gpsimd", "dma_start", inc=s_l, inck=16,
                      out=rowsB[h:h + 1, :], in_=pexp_sb[64:65, :])
                pr.wait("vector", s_sc)
                for tt in range(NT):
                    pr.op("vector", "tensor_copy",
                          out=yo_sb[64 * hi:64 * hi + 64,
                                    hp * T + tt * 512: hp * T + tt * 512 + 512],
                          in_=pexp_sb[0:64, tt * 512:(tt + 1) * 512],
                          inc=(s_ve if tt == 3 else None))
            barrier()
        tap_store("c0neg", c0neg[:])
        tap_store("ecoltap", ecol[:])
        tap_store("qp7", qp_sb[:])
        tap_store("kp7", kp_sb[:])
        tap_store("vp7", vp_sb[:])
        tap_store("ouns", yo_sb[:])
        tap_store("lall", rowsB[0:16, :])

        if lvl < 4:
            return
        pr.wait("vector", s_l)
        pr.op("vector", "reciprocal", out=rowsB[0:16, :], in_=rowsB[0:16, :], inc=s_ve)
        pr.wait("scalar", s_ve)
        pr.op("scalar", "copy", out=pexp_sb[0:16, :], in_=rowsB[0:16, :], inc=s_sc)
        for h in range(H):
            hp, hi = h // 2, h % 2
            pr.op("gpsimd", "dma_start", inc=s_gp, inck=16,
                  waits=[(s_sc, s_sc.n), (s_mm, s_mm.n)],
                  out=pexp_sb[64:65, :], in_=pexp_sb[h:h + 1, :])
            pr.wait("tensor", s_gp)
            pr.wait("tensor", s_ve)
            for tt in range(NT):
                pr.op("tensor", "matmul", SB(tt)[0:64, :], onesr_b3[64:65, 0:64],
                      pexp_sb[64:65, tt * 512:(tt + 1) * 512], start=True, stop=True,
                      inc=(s_mm if tt == 3 else None))
            pr.wait("vector", s_mm)
            for tt in range(NT):
                dst = yo_sb[64 * hi:64 * hi + 64,
                            hp * T + tt * 512: hp * T + tt * 512 + 512]
                pr.op("vector", "tensor_tensor", out=dst, in0=dst,
                      in1=SB(tt)[0:64, :], op=ALU.mult,
                      inc=(s_ve if tt == 3 else None))
        barrier()
        tap_store("onrm", yo_sb[:])

        if lvl < 5:
            return
        for o2c in range(8):
            gpw = [] if o2c < 2 else [(s_ve, s_ve.n)]
            gpld = s_gp.n
            pr.op("gpsimd", "dma_start", inc=s_gp, inck=16, waits=gpw,
                  out=ring[:, (o2c % 2) * 2048: (o2c % 2) * 2048 + 1024]
                  .rearrange("p (c j) -> p c j", c=8),
                  in_=woutT_h[:, o2c * 128:(o2c + 1) * 128]
                  .rearrange("(c p) j -> p c j", p=128))
            pr.wait("tensor", s_gp, gpld + 16)
            for tt in range(NT):
                bank = OB(tt % 2)
                wbk = [(s_ve, s_ve.n)] if (tt >= 2 or o2c > 0) else []
                for c in range(NC8):
                    pr.op("tensor", "matmul", bank,
                          ring[:, (o2c % 2) * 2048 + c * 128:
                               (o2c % 2) * 2048 + c * 128 + 128],
                          yo_sb[:, c * T + tt * 512: c * T + tt * 512 + 512],
                          start=(c == 0), stop=(c == 7),
                          waits=(wbk if c == 0 else []),
                          inc=(s_mm if c == 7 else None))
                sslot = stg if tt % 2 == 0 else stg2
                wv_ = [(s_mm, s_mm.n)]
                if tt >= 2 or o2c > 0:
                    wv_.append((s_gp, s_gp.n - 1))
                pr.op("vector", "tensor_scalar", out=sslot[:], in0=bank,
                      scalar1=cols[:, COL["bout"] + o2c:COL["bout"] + o2c + 1],
                      scalar2=None, op0=ALU.add, waits=wv_, inc=s_ve)
                pr.op("gpsimd", "tensor_tensor", out=xch(o2c, tt * TW, TW),
                      in0=xch(o2c, tt * TW, TW), in1=sslot[:], op=ALU.add,
                      waits=[(s_ve, s_ve.n)], inc=s_gp)
        barrier()
        tap_store("x1", x_sb[:])

    def emit_L1(s, scanA, scanB, sqbf):
        if lvl < 6:
            return
        emit_stats(sqbf)
        barrier()
        for tt in range(NT):
            t0 = tt * TW
            tsl = slice(t0, t0 + TW)
            pr.op("tensor", "matmul", SB(0), onesr_f[:], RA(0, tsl),
                  start=True, stop=True)
            pr.op("tensor", "matmul", SB(1), onesr3[32:33, 0:128], RA(32, tsl),
                  start=True, stop=True, inc=s_mm)
            pr.wait("vector", s_mm)
            for c in range(NC8):
                pr.op("vector", "tensor_tensor", out=stg[:], in0=xch(c, t0, TW),
                      in1=SB(0), op=ALU.subtract)
                pr.op("vector", "tensor_tensor", out=xnch(c, t0, TW), in0=stg[:],
                      in1=SB(1), op=ALU.mult,
                      inc=(s_ve if c == NC8 - 1 else None))
            pr.wait("tensor", s_ve)
        barrier()
        tap_store("xn1", xn_sb[:])

        if lvl < 7:
            return
        pr.op("gpsimd", "dma_start", inc=s_gp, inck=16,
              out=ring[:, 4096:4480].rearrange("p (c j) -> p c j", c=8),
              in_=win_h[:, 2 * D: 2 * D + 48].rearrange("(c p) j -> p c j", p=128))
        pr.wait("tensor", s_gp)
        sdst = [scanA[0:16, :], scanB[0:16, :], rowsA[0:16, :]]
        for j, dst in enumerate(sdst):
            wbk = [(s_ve, s_ve.n)] if j > 0 else []
            for tt in range(NT):
                for c in range(NC8):
                    pr.op("tensor", "matmul",
                          S_ps[0:16, tt * 512:(tt + 1) * 512],
                          ring[:, 4096 + c * 48 + 16 * j: 4096 + c * 48 + 16 * j + 16],
                          xnch(c, tt * TW, TW), start=(c == 0), stop=(c == 7),
                          waits=(wbk if (tt == 0 and c == 0) else []),
                          inc=(s_mm if (tt == 3 and c == 7) else None))
            pr.wait("vector", s_mm)
            pr.op("vector", "tensor_scalar", out=dst,
                  in0=S_ps[0:16, :], scalar1=sc16[0:16, j:j + 1],
                  scalar2=None, op0=ALU.add, inc=s_ve)
        bcd_ve = s_ve.n
        dl = rowsA[0:16, :]
        tmp = rowsB[0:16, :]
        pr.op("scalar", "activation", out=tmp, in_=dl, func=AF.Abs,
              waits=[(s_ve, bcd_ve)])
        pr.op("scalar", "activation", out=tmp, in_=tmp, func=AF.Exp, scale=-1.0)
        pr.op("scalar", "activation", out=tmp, in_=tmp, func=AF.Ln, bias=1.0,
              inc=s_sc)
        pr.wait("vector", s_sc)
        pr.op("vector", "tensor_scalar", out=dl, in0=dl,
              scalar1=0.0, scalar2=None, op0=ALU.max, inc=s_ve)
        pr.wait("vector", s_ve)
        pr.op("vector", "tensor_tensor", out=dl, in0=dl, in1=tmp, op=ALU.add,
              inc=s_ve)
        pr.wait("vector", s_ve)
        pr.op("vector", "tensor_scalar", out=tmp, in0=dl,
              scalar1=sc16[0:16, 4:5], scalar2=None, op0=ALU.mult, inc=s_ve)
        pr.wait("scalar", s_ve)
        pr.op("scalar", "activation", out=tmp, in_=tmp, func=AF.Exp, inc=s_sc)
        pr.wait("vector", s_sc)
        pr.op("vector", "tensor_tensor_scan", out=dl,
              data0=tmp, data1=scanA[0:16, :], initial=0.0,
              op0=ALU.mult, op1=ALU.add, inc=s_ve)
        pr.wait("vector", s_ve)
        pr.op("vector", "tensor_tensor", out=scanB[0:16, :],
              in0=scanB[0:16, :], in1=dl, op=ALU.mult, inc=s_ve)
        scan_ve = s_ve.n

        for pt in range(8):
            gpw = [] if pt < 2 else [(s_sc, s_sc.n)]
            gpld = s_gp.n
            pr.op("gpsimd", "dma_start", inc=s_gp, inck=16, waits=gpw,
                  out=ring[:, (pt % 2) * 2048:(pt % 2) * 2048 + 1024]
                  .rearrange("p (c j) -> p c j", c=8),
                  in_=win_h[:, D + pt * 128: D + (pt + 1) * 128]
                  .rearrange("(c p) j -> p c j", p=128))
            pr.wait("tensor", s_gp, gpld + 16)
            for tt in range(NT):
                bank = OB(tt % 2)
                wbk = [(s_sc, s_sc.n)] if tt >= 2 or pt > 0 else []
                for c in range(NC8):
                    pr.op("tensor", "matmul", bank,
                          ring[:, (pt % 2) * 2048 + c * 128:
                               (pt % 2) * 2048 + c * 128 + 128],
                          xnch(c, tt * TW, TW), start=(c == 0), stop=(c == 7),
                          waits=(wbk if c == 0 else []),
                          inc=(s_mm if c == 7 else None))
                pr.wait("scalar", s_mm)
                pr.op("scalar", "activation",
                      out=yo_sb[:, pt * T + tt * 512: pt * T + tt * 512 + 512],
                      in_=bank, func=AF.Gelu,
                      bias=cols[:, COL["binz"] + pt:COL["binz"] + pt + 1],
                      inc=s_sc)
        gate_sc = s_sc.n
        pr.wait("tensor", s_ve, scan_ve)
        for b in range(4):
            pr.op("tensor", "matmul", S_ps[0:1, b * 512:(b + 1) * 512],
                  sc16[0:16, 5:6], scanB[0:16, b * 512:(b + 1) * 512],
                  start=True, stop=True, inc=(s_mm if b == 3 else None))
        pr.wait("vector", s_mm)
        pr.op("vector", "tensor_copy", out=RB(32, SL), in_=S_ps[0:1, :],
              inc=s_ve)
        tap_store("scanS", rowsA[0:16, :])
        tap_store("yrow", rowsB[32:33, :])

        if lvl < 9:
            return
        pr.wait("vector", s_sc, gate_sc)
        pr.wait("tensor", s_ve)
        tap_store("gtap", yo_sb[:])
        for tt in range(NT):
            pr.op("tensor", "matmul", SB(0), onesr3[32:33, 0:128],
                  RB(32, slice(tt * 512, tt * 512 + 512)), start=True,
                  stop=True, inc=s_mm)
            pr.wait("vector", s_mm)
            for c in range(NC8):
                dst = yo_sb[:, c * T + tt * 512: c * T + tt * 512 + 512]
                pr.op("vector", "tensor_tensor", out=dst, in0=dst, in1=SB(0),
                      op=ALU.mult, inc=(s_ve if c == NC8 - 1 else None))
            pr.wait("tensor", s_ve)
        barrier()
        tap_store("utap", yo_sb[:])

        if lvl < 10:
            return
        for o2c in range(8):
            gpw = [] if o2c < 2 else [(s_ve, s_ve.n)]
            gpld = s_gp.n
            pr.op("gpsimd", "dma_start", inc=s_gp, inck=16, waits=gpw,
                  out=ring[:, (o2c % 2) * 2048:(o2c % 2) * 2048 + 1024]
                  .rearrange("p (c j) -> p c j", c=8),
                  in_=wout1_h[:, o2c * 128:(o2c + 1) * 128]
                  .rearrange("(c p) j -> p c j", p=128))
            pr.wait("tensor", s_gp, gpld + 16)
            for tt in range(NT):
                bank = OB(tt % 2)
                wbk = [(s_ve, s_ve.n)] if (tt >= 2 or o2c > 0) else []
                for c in range(NC8):
                    pr.op("tensor", "matmul", bank,
                          ring[:, (o2c % 2) * 2048 + c * 128:
                               (o2c % 2) * 2048 + c * 128 + 128],
                          yo_sb[:, c * T + tt * 512: c * T + tt * 512 + 512],
                          start=(c == 0), stop=False,
                          waits=(wbk if c == 0 else []))
                for c in range(NC8):
                    pr.op("tensor", "matmul", bank,
                          wbar_sb[:, c * D + o2c * 128: c * D + (o2c + 1) * 128],
                          xnch(c, tt * TW, TW), start=False, stop=(c == 7),
                          inc=(s_mm if c == 7 else None))
                sslot = stg if tt % 2 == 0 else stg2
                wv_ = [(s_mm, s_mm.n)]
                if tt >= 2 or o2c > 0:
                    wv_.append((s_gp, s_gp.n - 1))
                pr.op("vector", "tensor_scalar", out=sslot[:], in0=bank,
                      scalar1=cols[:, COL["badd"] + o2c:COL["badd"] + o2c + 1],
                      scalar2=None, op0=ALU.add, waits=wv_, inc=s_ve)
                pr.op("gpsimd", "tensor_tensor", out=xch(o2c, tt * TW, TW),
                      in0=xch(o2c, tt * TW, TW), in1=sslot[:], op=ALU.add,
                      waits=[(s_ve, s_ve.n)], inc=s_gp)
        barrier()

    for s in range(NSEQ):
        if do_l0:
            stA = contextlib.ExitStack()
            qp_sb = stA.enter_context(nc.sbuf_tensor(f"qp{s}", [128, T], BF16))
            kp_sb = stA.enter_context(nc.sbuf_tensor(f"kp{s}", [128, T], BF16))
            vp_sb = stA.enter_context(nc.sbuf_tensor(f"vp{s}", [128, 2304], BF16))
            h2_sb = stA.enter_context(
                nc.sbuf_tensor(f"h2{s}", [128, NC8 * TW], BF16))
            emit_L0(s, qp_sb, kp_sb, vp_sb, h2_sb)
            stA.close()
        else:
            if s > 0:
                pr.op("sync", "dma_start", inc=s_ld, inck=16,
                      waits=[(s_st, s_st.n)],
                      out=x_sb[:].rearrange("p (c t) -> p c t", c=NC8),
                      in_=x_h[s].rearrange("(c p) t -> p c t", p=128))
            barrier()
        if do_l1:
            stB = contextlib.ExitStack()
            scanA = stB.enter_context(nc.sbuf_tensor(f"scA{s}", [128, T], F32))
            scanB = stB.enter_context(nc.sbuf_tensor(f"scB{s}", [128, T], F32))
            sqbf = stB.enter_context(nc.sbuf_tensor(f"sq{s}", [128, T], BF16))
            emit_L1(s, scanA, scanB, sqbf)
            stB.close()
        barrier()
        pr.op("sync", "dma_start", inc=s_st, inck=16,
              out=out_h[s].rearrange("(c p) t -> p c t", p=128),
              in_=x_sb[:].rearrange("p (c t) -> p c t", c=NC8))
    pr.wait("sync", s_st)

    with nc.Block() as block:
        for ename in Prog.ENGS:
            fns = pr.items[ename]
            if not fns:
                continue

            def body(eng, fns=fns):
                for fn in fns:
                    fn(eng)
            getattr(block, ename)(body)

    stk.close()
    return nc, tap_hs


def _mk_smallrow(inp):
    sr = np.zeros((1, SR_LEN), np.float32)
    vals = {"bout": inp["L0_attn_bout"], "gamma": inp["L0_ffn_ln_gamma"],
            "beta": inp["L0_ffn_ln_beta"], "fc1b": inp["L0_ffn_fc1_bias"],
            "fc2b": inp["L0_ffn_fc2_bias"],
            "binz": inp["L1_attn_b_in"][D:2 * D],
            "binB": inp["L1_attn_b_in"][2 * D:2 * D + 16],
            "binC": inp["L1_attn_b_in"][2 * D + 16:2 * D + 32],
            "binD": inp["L1_attn_b_in"][2 * D + 32:2 * D + 48],
            "alog": inp["L1_attn_A_log"], "bout1": inp["L1_attn_b_out"]}
    for name, v in vals.items():
        off, ln = _SR[name]
        sr[0, off:off + ln] = np.asarray(v, np.float32)
    return sr


def _consts():
    band = np.zeros((128, 896), np.float32)
    for i in range(128):
        band[i, i + 384:] = 1.0
    return {"i128f": np.eye(128, dtype=np.float32),
            "bandbf": band.astype(ml_dtypes.bfloat16)}


_FUSED = True
_TAPS = ()


def _host_inputs(inp):
    inp = {k: np.asarray(v) for k, v in inp.items()}
    x = np.ascontiguousarray(inp["L0_attn_x"].astype(np.float32))
    common = dict(smallrow=_mk_smallrow(inp), **_consts())
    l0w = dict(
        wqkvT=np.ascontiguousarray(inp["L0_attn_Wqkv"].T.astype(np.float32)),
        woutT=np.ascontiguousarray(inp["L0_attn_Wout"].T.astype(np.float32)),
        fc1q=np.ascontiguousarray(inp["L0_ffn_fc1_q"].astype(np.uint8)),
        fc2q=np.ascontiguousarray(inp["L0_ffn_fc2_q"].astype(np.uint8)),
        fc1s=np.ascontiguousarray(inp["L0_ffn_fc1_scales"].astype(np.float32)),
        fc2s=np.ascontiguousarray(inp["L0_ffn_fc2_scales"].astype(np.float32)))
    l1w = dict(
        win=np.ascontiguousarray(inp["L1_attn_W_in"].astype(np.float32)),
        wout1=np.ascontiguousarray(inp["L1_attn_W_out"].astype(np.float32)),
        l1w=np.ascontiguousarray(
            inp["L1_ffn_W"].astype(np.float32).reshape(32 * D, D)),
        l1bf=np.ascontiguousarray(
            inp["L1_ffn_b"].astype(np.float32).reshape(1, 32 * D)),
        ids=np.ascontiguousarray(inp["L1_ffn_ids"].astype(np.int32)),
        wts=np.ascontiguousarray(
            inp["L1_ffn_weights"].astype(np.float32).reshape(1, 8)))
    shards = [np.ascontiguousarray(x[2 * c:2 * c + 2].transpose(0, 2, 1))
              for c in range(NCORES)]
    return shards, common, l0w, l1w


def kernel(**inp):
    shards, common, l0w, l1w = _host_inputs(inp)
    if _FUSED:
        nc, _ = build(("L0", "L1"), taps=_TAPS)
        in_maps = [dict(xT=shards[c], **common, **l0w, **l1w)
                   for c in range(NCORES)]
        res = run_bass_kernel_spmd(nc, in_maps, core_ids=list(range(NCORES)))
        outs = [res.results[c]["outT"] for c in range(NCORES)]
    else:
        ncA, _ = build(("L0",), taps=_TAPS)
        in_maps = [dict(xT=shards[c], **common, **l0w) for c in range(NCORES)]
        resA = run_bass_kernel_spmd(ncA, in_maps, core_ids=list(range(NCORES)))
        x1s = [np.asarray(resA.results[c]["outT"]) for c in range(NCORES)]
        ncB, _ = build(("L1",), taps=_TAPS)
        in_mapsB = [dict(xT=np.ascontiguousarray(x1s[c]), **common, **l1w)
                    for c in range(NCORES)]
        resB = run_bass_kernel_spmd(ncB, in_mapsB, core_ids=list(range(NCORES)))
        outs = [resB.results[c]["outT"] for c in range(NCORES)]
    out = np.empty((16, T, D), np.float32)
    for c in range(NCORES):
        out[2 * c:2 * c + 2] = np.asarray(outs[c], np.float32).transpose(0, 2, 1)
    return out


# version 13
# speedup vs baseline: 1.1779x; 1.1779x over previous
import contextlib
import numpy as np
import ml_dtypes
import concourse.bass as bass
import concourse.mybir as mybir
from concourse.bass_utils import run_bass_kernel_spmd

F32 = mybir.dt.float32
BF16 = mybir.dt.bfloat16
I32 = mybir.dt.int32
AF = mybir.ActivationFunctionType
ALU = mybir.AluOpType
AXX = mybir.AxisListType.X

NCORES = 8
NSEQ = 2
D = 1024
T = 2048
H = 16
F4 = 4096
NT = 4
TW = 512
NC8 = 8
EPS = 1e-5

_SR = {}
def _mk_sr():
    off = 0
    for name, ln in [("bout", D), ("gamma", D), ("beta", D), ("fc1b", F4),
                     ("fc2b", D), ("binz", D), ("binB", 16), ("binC", 16),
                     ("binD", 16), ("alog", 16), ("bout1", D)]:
        if ln >= 128:
            off = ((off + 127) // 128) * 128
        _SR[name] = (off, ln)
        off += ln
    return ((off + 2047) // 2048) * 2048
SR_LEN = _mk_sr()


class Prog:
    ENGS = ("sync", "gpsimd", "scalar", "vector", "tensor")

    def __init__(self):
        self.items = {e: [] for e in self.ENGS}

    def raw(self, eng, fn):
        self.items[eng].append(fn)

    def op(self, eng, method, *args, inc=None, inck=1, waits=(), **kw):
        waits = [(c.h, v) for (c, v) in waits]
        inch = None
        if inc is not None:
            inc.n += inck
            inch = inc.h

        def fn(e, waits=waits, method=method, args=args, kw=kw, inch=inch,
               inck=inck):
            for h, v in waits:
                e.wait_ge(h, v)
            instr = getattr(e, method)(*args, **kw)
            if inch is not None:
                instr.then_inc(inch, inck)
            return instr
        self.items[eng].append(fn)

    def wait(self, eng, ctr, val=None):
        v = ctr.n if val is None else val
        self.items[eng].append(lambda e, h=ctr.h, v=v: e.wait_ge(h, v))


class Ctr:
    def __init__(self, stk, nc, name):
        self.h = stk.enter_context(nc.semaphore(name))
        self.n = 0


def build(phases=("L0", "L1"), x_in_name="xT", out_name="outT", taps=(), lvl=99):
    nc = bass.Bass("TRN2", target_bir_lowering=False, debug=False)
    stk = contextlib.ExitStack()
    taps = set(taps)
    do_l0 = "L0" in phases
    do_l1 = "L1" in phases
    pr = Prog()

    def din(name, shape, dt=F32):
        return nc.dram_tensor(name, shape, dt, kind="ExternalInput").ap()

    x_h = din(x_in_name, [NSEQ, D, T])
    out_h = nc.dram_tensor(out_name, [NSEQ, D, T], F32, kind="ExternalOutput").ap()
    sr_h = din("smallrow", [1, SR_LEN])
    i128f_h = din("i128f", [128, 128])
    band_h = din("bandbf", [128, 896], BF16)
    if do_l0:
        wqkvT_h = din("wqkvT", [D, 3 * D])
        woutT_h = din("woutT", [D, D])
        fc1q_h = din("fc1q", [D, F4], mybir.dt.uint8)
        fc2q_h = din("fc2q", [F4, D], mybir.dt.uint8)
        fc1s_h = din("fc1s", [8, F4])
        fc2s_h = din("fc2s", [32, D])
        w1s = nc.dram_tensor("w1s", [D, F4], BF16, kind="Internal").ap()
        w2s = nc.dram_tensor("w2s", [F4, D], BF16, kind="Internal").ap()
    if do_l1:
        win_h = din("win", [D, 2096])
        wout1_h = din("wout1", [D, D])
        l1w_h = din("l1w", [32 * D, D])
        l1bf_h = din("l1bf", [1, 32 * D])
        ids_h = din("ids", [8], I32)
        wts_h = din("wts", [1, 8])
        wsel = nc.dram_tensor("wsel", [8 * D, D], F32, kind="Internal").ap()

    tap_hs = {}

    s_ld = Ctr(stk, nc, "s_ld")
    s_gp = Ctr(stk, nc, "s_gp")
    s_mm = Ctr(stk, nc, "s_mm")
    s_sc = Ctr(stk, nc, "s_sc")
    s_ve = Ctr(stk, nc, "s_ve")
    s_st = Ctr(stk, nc, "s_st")
    s_l = Ctr(stk, nc, "s_l")
    s_id = Ctr(stk, nc, "s_id")
    ALLC = [s_ld, s_gp, s_mm, s_sc, s_ve]

    def barrier():
        for e in Prog.ENGS:
            for c in ALLC:
                pr.wait(e, c)

    def tap_store(name, ap):
        if name not in taps or name in tap_hs:
            return
        barrier()
        tap_hs[name] = nc.dram_tensor("tap_" + name, list(ap.shape), ap.dtype,
                                      kind="ExternalOutput").ap()
        pr.op("sync", "dma_start", inc=s_st, inck=16,
              out=tap_hs[name][:, :], in_=ap)
        barrier()

    def tap_dram(name, dap):
        if name not in taps or name in tap_hs:
            return
        barrier()
        tap_hs[name] = nc.dram_tensor("tap_" + name, list(dap.shape), dap.dtype,
                                      kind="ExternalOutput").ap()
        pr.op("gpsimd", "dma_start", inc=s_gp, inck=16,
              out=tap_hs[name][:, :], in_=dap)
        barrier()

    sbP = lambda n, sh, dt: stk.enter_context(nc.sbuf_tensor(n, sh, dt))
    cols = sbP("cols", [128, 80], F32)
    COL = {"bout": 0, "gamma": 8, "beta": 16, "fc1b": 24, "fc2b5": 56,
           "binz": 64, "badd": 72}
    i128f = sbP("i128f_t", [128, 128], F32)
    band_t = sbP("band_t", [128, 896], BF16)
    onesc_bf = sbP("onesc_bf", [128, 1], BF16)
    onesr_f = sbP("onesr_f", [1, 128], F32)
    onesr3 = sbP("onesr3", [65, 128], F32)
    halfr = sbP("halfr", [65, 128], F32)
    onesr_b3 = sbP("onesr_b3", [65, 128], BF16)
    sca = sbP("sca", [1, 16], F32)
    epsc = sbP("epsc", [1, 2], F32)
    c0neg = sbP("c0neg", [128, 1], F32)
    ecol = sbP("ecol", [128, 1], F32)
    sc16 = sbP("sc16", [16, 8], F32)
    if do_l1:
        wbar_sb = sbP("wbar_sb", [128, 8 * D], BF16)
        wtsc_sb = sbP("wtsc", [8, 1], F32)
        wrep_sb = sbP("wrep", [128, 8], F32)
        ids_sb = sbP("ids_sb", [1, 8], I32)

    S_ps = stk.enter_context(nc.psum_tensor("S_ps", [128, T], F32))
    O_ps = stk.enter_context(nc.psum_tensor("O_ps", [128, T], F32))
    SB = lambda b: S_ps[:, b * 512:(b + 1) * 512]
    OB = lambda b: O_ps[:, b * 512:(b + 1) * 512]

    x_sb = stk.enter_context(nc.sbuf_tensor("x_sb", [128, NC8 * T], F32))
    rowsA = stk.enter_context(nc.sbuf_tensor("rowsA", [128, T], F32))
    rowsB = stk.enter_context(nc.sbuf_tensor("rowsB", [128, T], F32))
    pexp_sb = stk.enter_context(nc.sbuf_tensor("pexp_sb", [128, T], BF16))
    stg = stk.enter_context(nc.sbuf_tensor("stg", [128, TW], F32))
    stg2 = stk.enter_context(nc.sbuf_tensor("stg2", [128, TW], F32))

    stkS = contextlib.ExitStack()
    sbS = lambda n, sh, dt: stkS.enter_context(nc.sbuf_tensor(n, sh, dt))
    bbrow = sbS("bbrow", [1, 1024], F32)
    if do_l1:
        bsel_sb = sbS("bsel_sb", [8, D], F32)
        wkI_sb = sbS("wkI_sb", [128, 8 * 128], F32)
        ringS = sbS("ringS", [128, 6 * D], F32)
    if do_l0:
        qint = sbS("qint", [128, 2 * 2048], mybir.dt.uint8)
        qbf = sbS("qbf", [128, 2 * 2048], BF16)
        wtmp = sbS("wtmp", [128, 2 * 2048], BF16)

    pr.op("sync", "dma_start", inc=s_ld, inck=16, out=i128f[:], in_=i128f_h[:])
    pr.op("sync", "dma_start", inc=s_ld, inck=16, out=band_t[:], in_=band_h[:])
    _SLOT = [("A", 0), ("A", 32), ("A", 64), ("B", 0), ("B", 32), ("B", 64)]

    def srow_ap(off, ln):
        sl = off // 2048
        kind, r = _SLOT[sl]
        tns = rowsA if kind == "A" else rowsB
        o = off % 2048
        return tns[r:r + 1, o:o + ln], i128f[r:r + 1, r:r + 1]
    for sl in range(6):
        kind, r = _SLOT[sl]
        tns = rowsA if kind == "A" else rowsB
        w = min(2048, SR_LEN - sl * 2048)
        pr.op("sync", "dma_start", inc=s_ld, inck=16,
              out=tns[r:r + 1, 0:w], in_=sr_h[0:1, sl * 2048: sl * 2048 + w])
    pr.op("sync", "dma_start", inc=s_ld, inck=16,
          out=x_sb[:].rearrange("p (c t) -> p c t", c=NC8),
          in_=x_h[0].rearrange("(c p) t -> p c t", p=128))
    xld0_done = s_ld.n
    pr.wait("vector", s_ld)
    pr.op("vector", "memset", onesc_bf[:], 1.0)
    pr.op("vector", "memset", onesr_f[:], 1.0)
    pr.op("vector", "memset", onesr3[:], 1.0)
    pr.op("vector", "memset", halfr[:], 0.5)
    pr.op("vector", "memset", onesr_b3[:], 1.0)
    if do_l1:
        pr.raw("vector", lambda e: e.memset(sc16[0:16, 5:6], 1.0))
    pr.op("vector", "memset", sca[:], 0.0)
    pr.op("vector", "memset", epsc[0:1, 0:1], EPS)
    pr.op("vector", "memset", epsc[0:1, 1:2], EPS * EPS, inc=s_ve)

    pr.wait("tensor", s_ld)
    pr.wait("tensor", s_ve)
    colspec = []
    if do_l0:
        colspec += [("bout", COL["bout"], 8), ("gamma", COL["gamma"], 8),
                    ("beta", COL["beta"], 8), ("fc1b", COL["fc1b"], 32),
                    ("fc2b", COL["fc2b5"], 8)]
    if do_l1:
        colspec += [("binz", COL["binz"], 8), ("bout1", COL["badd"], 8)]
    flat = [(n, b, c) for (n, b, ncols) in colspec for c in range(ncols)]
    for i0 in range(0, len(flat), 16):
        grp = flat[i0:i0 + 16]
        for j, (name, base, c) in enumerate(grp):
            off, _ = _SR[name]
            sap, idap = srow_ap(off + c * 128, 128)
            pr.op("tensor", "transpose", O_ps[0:128, j:j + 1],
                  sap, idap, inc=(s_mm if j == len(grp) - 1 else None))
        pr.wait("vector", s_mm)
        for j, (name, base, c) in enumerate(grp):
            sc = 0.5 if name == "fc2b" else 1.0
            pr.op("vector", "tensor_scalar", out=cols[:, base + c:base + c + 1],
                  in0=O_ps[0:128, j:j + 1], scalar1=sc, scalar2=None,
                  op0=ALU.mult, inc=(s_ve if j == len(grp) - 1 else None))
        pr.wait("tensor", s_ve)
    if do_l1:
        for j, name in enumerate(["binB", "binC", "binD", "alog"]):
            off, _ = _SR[name]
            sap, idap = srow_ap(off, 16)
            pr.op("tensor", "transpose", O_ps[0:16, j:j + 1],
                  sap, idap, inc=(s_mm if j == 3 else None))
        pr.wait("vector", s_mm)
        pr.op("vector", "tensor_copy", out=sc16[0:16, 0:4], in_=O_ps[0:16, 0:4],
              inc=s_ve)
        pr.wait("scalar", s_ve)
        pr.op("scalar", "activation", out=sc16[0:16, 4:5], in_=sc16[0:16, 3:4],
              func=AF.Exp, inc=s_sc)
        pr.wait("vector", s_sc)
        pr.op("vector", "tensor_scalar", out=sc16[0:16, 4:5],
              in0=sc16[0:16, 4:5], scalar1=-1.0, scalar2=None, op0=ALU.mult,
              inc=s_ve)
        pr.wait("tensor", s_ve)

    if do_l1:
        pr.op("gpsimd", "dma_start", inc=s_gp, inck=16, out=ids_sb[:],
              in_=ids_h[:])
        pr.wait("gpsimd", s_gp)
        pr.op("gpsimd", "tensor_scalar", out=ids_sb[:], in0=ids_sb[:],
              scalar1=D, scalar2=None, op0=ALU.mult, inc=s_id)
        pr.wait("gpsimd", s_id)

        def emit_gather(e):
            _, vals = nc.values_load_multi_w_load_instructions(
                ids_sb[0:1, 0:8], engines=[mybir.EngineType.Pool],
                skip_runtime_bounds_check=True)
            for k in range(8):
                e.dma_start(out=wsel[k * D:(k + 1) * D, :],
                            in_=l1w_h[bass.ds(vals[k], D), :]).then_inc(s_gp.h, 16)
                e.dma_start(
                    out=bsel_sb[k:k + 1, :],
                    in_=l1bf_h[0:1, bass.ds(vals[k], D)]
                ).then_inc(s_gp.h, 16)
        gather_mark = {}
        g0 = s_gp.n
        for k in range(8):
            gather_mark[k] = g0 + 32 * (k + 1)
        pr.raw("gpsimd", emit_gather)
        s_gp.n += 16 * 16
        pr.op("gpsimd", "dma_start", inc=s_gp, inck=16, out=sca[0:1, 0:8],
              in_=wts_h[:], waits=[(s_ve, s_ve.n)])

        pr.wait("tensor", s_gp)
        pr.op("tensor", "transpose", O_ps[0:8, 16:17], sca[0:1, 0:8],
              i128f[0:1, 0:1])
        pr.op("tensor", "matmul", O_ps[0:128, 512:520], onesr_f[:],
              sca[0:1, 0:8], start=True, stop=True, inc=s_mm)
        pr.wait("vector", s_mm)
        pr.op("vector", "tensor_copy", out=wtsc_sb[:], in_=O_ps[0:8, 16:17])
        pr.op("vector", "tensor_copy", out=wrep_sb[:], in_=O_ps[0:128, 512:520],
              inc=s_ve)
        pr.wait("vector", s_ve)
        for k in range(8):
            pr.op("vector", "tensor_scalar", out=wkI_sb[:, k * 128:(k + 1) * 128],
                  in0=i128f[:], scalar1=wrep_sb[:, k:k + 1], scalar2=None,
                  op0=ALU.mult, inc=(s_ve if k == 7 else None))

        pr.wait("tensor", s_ve)
        pr.op("tensor", "matmul", O_ps[0:1, 1024:1536], wtsc_sb[:],
              bsel_sb[:, 0:512], start=True, stop=True)
        pr.op("tensor", "matmul", O_ps[0:1, 1536:2048], wtsc_sb[:],
              bsel_sb[:, 512:1024], start=True, stop=True, inc=s_mm)
        pr.wait("scalar", s_mm)
        pr.op("scalar", "copy", out=bbrow[0:1, 0:1024], in_=O_ps[0:1, 1024:2048],
              inc=s_sc)
        pr.wait("tensor", s_sc)
        for c in range(8):
            pr.op("tensor", "transpose", S_ps[0:128, c:c + 1],
                  bbrow[0:1, c * 128:(c + 1) * 128], i128f[0:1, 0:1],
                  inc=(s_mm if c == 7 else None))
        pr.wait("vector", s_mm)
        pr.op("vector", "tensor_tensor", out=cols[:, COL["badd"]:COL["badd"] + 8],
              in0=cols[:, COL["badd"]:COL["badd"] + 8], in1=S_ps[0:128, 0:8],
              op=ALU.add, inc=s_ve)

        barrier()
        RN = 6
        gp0, mm0, ve0 = s_gp.n, s_mm.n, s_ve.n
        for oc in range(8):
            for k in range(8):
                idx = oc * 8 + k
                slot = idx % RN
                w = [] if idx < RN else [(s_mm, mm0 + (idx - RN + 1))]
                w.append((s_gp, gather_mark[k]))
                pr.op("gpsimd", "dma_start", inc=s_gp, inck=16, waits=w,
                      out=ringS[:, slot * D:(slot + 1) * D],
                      in_=wsel[k * D + oc * 128: k * D + (oc + 1) * 128, :])
        for oc in range(8):
            for k in range(8):
                idx = oc * 8 + k
                slot = idx % RN
                w = [(s_gp, gp0 + 16 * (idx + 1))]
                if k == 0 and oc > 0:
                    w.append((s_ve, ve0 + oc))
                for dc in range(8):
                    pr.op("tensor", "matmul", S_ps[:, dc * 128:(dc + 1) * 128],
                          ringS[:, slot * D + dc * 128: slot * D + (dc + 1) * 128],
                          wkI_sb[:, k * 128:(k + 1) * 128],
                          start=(k == 0), stop=(k == 7),
                          waits=(w if dc == 0 else []),
                          inc=(s_mm if dc == 7 else None))
            pr.wait("vector", s_mm, mm0 + (oc + 1) * 8)
            for dc in range(8):
                pr.op("vector", "tensor_copy",
                      out=wbar_sb[:, dc * D + oc * 128: dc * D + (oc + 1) * 128],
                      in_=S_ps[:, dc * 128:(dc + 1) * 128],
                      inc=(s_ve if dc == 7 else None))

    if do_l0:
        jobs = ([("fc1", g, hh, 2048) for g in range(8) for hh in range(2)]
                + [("fc2", g, 0, 1024) for g in range(32)])
        sc0, mm0, ve0 = s_sc.n, s_mm.n, s_ve.n
        qmark = {}
        stmark = {}
        for j, (kind, g, hh, w) in enumerate(jobs):
            slot = j % 2
            wq = [] if j < 2 else [(s_sc, sc0 + (j - 1))]
            src = (fc1q_h[g * 128:(g + 1) * 128, hh * 2048:(hh + 1) * 2048]
                   if kind == "fc1" else fc2q_h[g * 128:(g + 1) * 128, :])
            qmark[j] = s_ld.n + 16
            pr.op("sync", "dma_start", inc=s_ld, inck=16, waits=wq,
                  out=qint[:, slot * 2048: slot * 2048 + w], in_=src)
            wr = [(s_mm, mm0)] if j == 0 else [(s_mm, mm0 + j)]
            srcs = (fc1s_h[g:g + 1, hh * 2048:(hh + 1) * 2048] if kind == "fc1"
                    else fc2s_h[g:g + 1, :])
            rowmark = ("gp", s_gp.n + 16)
            pr.op("gpsimd", "dma_start", inc=s_gp, inck=16, waits=wr,
                  out=rowsB[64:65, 0:w], in_=srcs)
            wc = [(s_ld, qmark[j])]
            if j >= 2:
                wc.append((s_ve, ve0 + (j - 1)))
            pr.op("scalar", "activation", out=qbf[:, slot * 2048: slot * 2048 + w],
                  in_=qint[:, slot * 2048: slot * 2048 + w], func=AF.Copy,
                  bias=-128.0, waits=wc, inc=s_sc)
            wt = [(s_gp, rowmark[1])]
            if j > 0:
                wt.append((s_ve, ve0 + j))
            else:
                wt.append((s_ve, ve0))
                wt.append((s_sc, sc0))
            repl = onesr3[64:65, :] if kind == "fc1" else halfr[64:65, :]
            for b in range(w // 512):
                pr.op("tensor", "matmul", OB(b), repl,
                      rowsB[64:65, b * 512:(b + 1) * 512], start=True, stop=True,
                      waits=(wt if b == 0 else []),
                      inc=(s_mm if b == w // 512 - 1 else None))
            wm = [(s_sc, sc0 + j + 1), (s_mm, mm0 + j + 1)]
            if j >= 2:
                wm.append((s_ld, stmark[j - 2]))
            pr.op("vector", "tensor_tensor",
                  out=wtmp[:, slot * 2048: slot * 2048 + w],
                  in0=qbf[:, slot * 2048: slot * 2048 + w],
                  in1=O_ps[:, 0:w], op=ALU.mult, waits=wm, inc=s_ve)
            dst = (w1s[g * 128:(g + 1) * 128, hh * 2048:(hh + 1) * 2048]
                   if kind == "fc1" else w2s[g * 128:(g + 1) * 128, :])
            stmark[j] = s_ld.n + 16
            pr.op("sync", "dma_start", inc=s_ld, inck=16,
                  waits=[(s_ve, ve0 + j + 1)],
                  out=dst, in_=wtmp[:, slot * 2048: slot * 2048 + w])

    barrier()
    if do_l0:
        tap_dram("w1sd", w1s)
        tap_dram("w2sd", w2s)

    stkS.close()
    sbM = lambda n, sh, dt: stk.enter_context(nc.sbuf_tensor(n, sh, dt))
    xn_sb = sbM("xn_sb", [128, NC8 * T], BF16)
    yo_sb = sbM("yo_sb", [128, NC8 * T], BF16)
    ring = sbM("ring", [128, 3 * 2048], BF16)

    lall = rowsB[0:16, :]
    lrow = rowsB[64:65, :]
    RA = lambda r, sl: rowsA[r:r + 1, sl]
    RB = lambda r, sl: rowsB[r:r + 1, sl]
    xch = lambda c, t0=0, w=T: x_sb[:, c * T + t0: c * T + t0 + w]
    xnch = lambda c, t0=0, w=T: xn_sb[:, c * T + t0: c * T + t0 + w]
    SL = slice(0, T)

    def emit_stats(sqstage):
        sc0 = s_sc.n
        ve0 = s_ve.n
        mm0 = s_mm.n
        for c in range(NC8):
            wcast = [] if c == 0 else [(s_mm, mm0 + 2 * c)]
            pr.op("vector", "tensor_copy", out=pexp_sb[:], in_=xch(c),
                  waits=wcast, inc=s_ve)
            pr.op("scalar", "activation", out=sqstage[:], in_=xch(c),
                  func=AF.Square, waits=wcast, inc=s_sc)
            pr.wait("tensor", s_ve, ve0 + c + 1)
            for b in range(4):
                pr.op("tensor", "matmul", S_ps[0:1, b * 512:(b + 1) * 512],
                      onesc_bf[:], pexp_sb[:, b * 512:(b + 1) * 512],
                      start=(c == 0), stop=(c == NC8 - 1),
                      inc=(s_mm if b == 3 else None))
            pr.wait("tensor", s_sc, sc0 + c + 1)
            for b in range(4):
                pr.op("tensor", "matmul", O_ps[0:1, b * 512:(b + 1) * 512],
                      onesc_bf[:], sqstage[:, b * 512:(b + 1) * 512],
                      start=(c == 0), stop=(c == NC8 - 1),
                      inc=(s_mm if b == 3 else None))
        pr.wait("vector", s_mm)
        pr.op("vector", "tensor_scalar", out=RA(0, SL), in0=S_ps[0:1, :],
              scalar1=1.0 / D, scalar2=None, op0=ALU.mult)
        pr.op("vector", "tensor_scalar", out=RB(64, SL), in0=O_ps[0:1, :],
              scalar1=1.0 / D, scalar2=None, op0=ALU.mult, inc=s_ve)
        pr.wait("vector", s_ve)
        pr.op("vector", "tensor_tensor", out=RA(64, SL), in0=RA(0, SL),
              in1=RA(0, SL), op=ALU.mult, inc=s_ve)
        pr.wait("vector", s_ve)
        pr.op("vector", "tensor_tensor", out=RA(64, SL), in0=RB(64, SL),
              in1=RA(64, SL), op=ALU.subtract, inc=s_ve)
        pr.wait("scalar", s_ve)
        pr.op("scalar", "activation", out=RB(0, SL), in_=RA(64, SL), func=AF.Ln,
              bias=epsc[0:1, 0:1])
        pr.op("scalar", "activation", out=RA(32, SL), in_=RB(0, SL), func=AF.Exp,
              scale=-0.5, inc=s_sc)

    def emit_L0(s, qp_sb, kp_sb, vp_sb, h2_sb):
        if s > 0:
            pr.op("sync", "dma_start", inc=s_ld, inck=16,
                  waits=[(s_st, s_st.n)],
                  out=x_sb[:].rearrange("p (c t) -> p c t", c=NC8),
                  in_=x_h[s].rearrange("(c p) t -> p c t", p=128))
        pr.wait("scalar", s_ld)
        pr.wait("tensor", s_ld)
        pr.wait("vector", s_ld)
        if lvl < 1:
            return
        emit_stats(h2_sb[:, 0:T])
        pr.op("scalar", "activation", out=pexp_sb[0:1, :], in_=RA(64, SL),
              func=AF.Ln, scale=1.0 + EPS, bias=epsc[0:1, 1:2], inc=s_sc)
        pr.wait("vector", s_sc)
        pr.op("vector", "tensor_tensor", out=RB(0, SL), in0=RB(0, SL),
              in1=pexp_sb[0:1, :], op=ALU.subtract, inc=s_ve)
        pr.wait("scalar", s_ve)
        pr.op("scalar", "activation", out=RB(0, SL), in_=RB(0, SL), func=AF.Exp,
              scale=0.5, inc=s_sc)
        barrier()

        if lvl < 2:
            return
        prev_zbase = None
        prev_fc2_end = None
        prev_gadd = None
        for tt in range(NT):
            t0 = tt * TW
            tsl = slice(t0, t0 + TW)
            wr0 = [] if prev_zbase is None else [(s_ve, prev_zbase + 1)]
            wr1 = [] if prev_zbase is None else [(s_ve, prev_zbase + 2)]
            wr2 = [] if prev_zbase is None else [(s_ve, prev_zbase + 3)]
            pr.op("tensor", "matmul", SB(0), onesr_f[:], RA(0, tsl),
                  start=True, stop=True, waits=wr0)
            pr.op("tensor", "matmul", SB(1), onesr3[32:33, 0:128], RA(32, tsl),
                  start=True, stop=True, waits=wr1)
            pr.op("tensor", "matmul", SB(2), onesr_f[:], RB(0, tsl),
                  start=True, stop=True, waits=wr2, inc=s_mm)
            pr.wait("vector", s_mm)
            if prev_gadd is not None:
                pr.wait("vector", s_gp, prev_gadd)
            ve_h2base = s_ve.n
            for c in range(NC8):
                pr.op("vector", "tensor_tensor", out=stg[:], in0=xch(c, t0, TW),
                      in1=SB(0), op=ALU.subtract)
                pr.op("vector", "tensor_tensor", out=stg2[:], in0=stg[:],
                      in1=SB(1), op=ALU.mult)
                pr.op("vector", "tensor_copy", out=xnch(c, t0, TW), in_=stg2[:])
                pr.op("vector", "tensor_tensor", out=xch(c, t0, TW),
                      in0=xch(c, t0, TW), in1=stg2[:], op=ALU.add)
                pr.op("vector", "tensor_tensor", out=stg[:], in0=xnch(c, t0, TW),
                      in1=SB(2), op=ALU.mult)
                pr.op("vector", "tensor_scalar",
                      out=h2_sb[:, c * TW:(c + 1) * TW], in0=stg[:],
                      scalar1=cols[:, COL["gamma"] + c:COL["gamma"] + c + 1],
                      scalar2=cols[:, COL["beta"] + c:COL["beta"] + c + 1],
                      op0=ALU.mult, op1=ALU.add, inc=s_ve)
            ve_h2 = s_ve.n
            if tt == 1:
                tap_store("h2t0", h2_sb[:])
            ld0 = s_ld.n
            sc0f = s_sc.n

            def w1slot(pld):
                return (ring, (pld % 4) * 2048) if pld % 4 < 3 else (pexp_sb, 0)
            for pld in range(16):
                if pld < 4:
                    w = [] if prev_fc2_end is None else [(s_mm, prev_fc2_end)]
                    if pld == 3 and tt == 0:
                        w += [(s_mm, s_mm.n), (s_ve, s_ve.n), (s_sc, s_sc.n),
                              (s_l, s_l.n), (s_gp, s_gp.n)]
                else:
                    w = [(s_sc, sc0f + 2 * (pld - 3))]
                st_, so_ = w1slot(pld)
                pr.op("sync", "dma_start", inc=s_ld, inck=16, waits=w,
                      out=st_[:, so_:so_ + 2048]
                      .rearrange("p (h j) -> p h j", h=8),
                      in_=w1s[:, pld * 256:(pld + 1) * 256]
                      .rearrange("(h p) j -> p h j", p=128))
            for fc in range(32):
                pld = fc // 2
                st_, so_ = w1slot(pld)
                slot = so_ + (fc % 2) * 128
                bank = SB(3) if fc % 2 == 0 else OB(0)
                wts_ = [(s_ld, ld0 + 16 * (pld + 1))]
                if fc >= 2:
                    wts_.append((s_sc, sc0f + fc - 1))
                elif prev_zbase is not None:
                    wts_.append((s_ve, prev_zbase + (4 if fc == 0 else 5)))
                for hc in range(NC8):
                    w2_ = list(wts_) if hc == 0 else []
                    if fc < 2:
                        w2_.append((s_ve, ve_h2base + hc + 1))
                    pr.op("tensor", "matmul", bank,
                          st_[:, slot + hc * 256: slot + hc * 256 + 128],
                          h2_sb[:, hc * TW:(hc + 1) * TW],
                          start=(hc == 0), stop=(hc == 7),
                          waits=w2_,
                          inc=(s_mm if hc == 7 else None))
                pr.wait("scalar", s_mm)
                pr.op("scalar", "activation",
                      out=yo_sb[:, fc * TW:(fc + 1) * TW], in_=bank,
                      func=AF.Silu,
                      bias=cols[:, COL["fc1b"] + fc:COL["fc1b"] + fc + 1],
                      inc=s_sc)
            if tt == 0:
                tap_store("yt0", yo_sb[:])
            ld1 = s_ld.n
            mm1 = s_mm.n
            sc1 = s_sc.n

            def w2slot(pld):
                return (ring, (pld % 4) * 2048) if pld % 4 < 3 else (h2_sb, 0)
            for pld in range(16):
                w = [(s_sc, sc1)] if pld < 4 else [(s_mm, mm1 + 2 * (pld - 3))]
                st_, so_ = w2slot(pld)
                pr.op("sync", "dma_start", inc=s_ld, inck=16, waits=w,
                      out=st_[:, so_:so_ + 2048]
                      .rearrange("p (q h) -> p q h", q=2),
                      in_=w2s[pld * 256:(pld + 1) * 256, :]
                      .rearrange("(q p) h -> p q h", p=128))
            for fc in range(32):
                pld = fc // 2
                st_, so_ = w2slot(pld)
                slot = so_ + (fc % 2) * 1024
                w = [(s_ld, ld1 + 16 * (pld + 1))]
                if fc == 0:
                    w.append((s_sc, sc1))
                for hc in range(NC8):
                    bank = SB(hc) if hc < 4 else OB(hc - 4)
                    pr.op("tensor", "matmul", bank,
                          st_[:, slot + hc * 128: slot + hc * 128 + 128],
                          yo_sb[:, fc * TW:(fc + 1) * TW],
                          start=(fc == 0), stop=(fc == 31),
                          waits=(w if hc == 0 else []),
                          inc=(s_mm if hc == 7 else None))
            prev_fc2_end = s_mm.n
            zb = s_ve.n
            gp_add0 = s_gp.n
            for hc in range(NC8):
                bank = SB(hc) if hc < 4 else OB(hc - 4)
                sslot = stg2 if hc % 2 == 0 else stg
                wv_ = [(s_mm, s_mm.n)]
                if hc >= 2:
                    wv_.append((s_gp, gp_add0 + hc - 1))
                pr.op("vector", "tensor_scalar", out=sslot[:], in0=bank,
                      scalar1=cols[:, COL["fc2b5"] + hc:COL["fc2b5"] + hc + 1],
                      scalar2=None, op0=ALU.add, waits=wv_, inc=s_ve)
                pr.op("gpsimd", "tensor_tensor", out=xch(hc, t0, TW),
                      in0=xch(hc, t0, TW), in1=sslot[:], op=ALU.add,
                      waits=[(s_ve, s_ve.n)], inc=s_gp)
            prev_zbase = zb
            prev_gadd = s_gp.n
        barrier()
        tap_store("xn", xn_sb[:])
        tap_store("cffn", x_sb[:])

        if lvl < 3:
            return
        pr.op("vector", "memset", vp_sb[:], 0.0, inc=s_ve)
        import os
        NHP = int(os.environ.get("K_NHP", "8"))
        NHI = int(os.environ.get("K_NHI", "2"))
        NKT = int(os.environ.get("K_NKT", "16"))
        KSUB = int(os.environ.get("K_SUB", "99"))
        gpmark = {}
        mm_pair_end = {}

        def qkv_loads(hp):
            pi = hp % 2
            gpmark[hp] = s_gp.n
            if hp < 2:
                w = [(s_mm, s_mm.n)]
            else:
                w = [(s_mm, mm_pair_end[hp - 2])]
            for j, colof in enumerate([hp * 128, D + hp * 128,
                                       2 * D + hp * 128]):
                pr.op("gpsimd", "dma_start", inc=s_gp, inck=16,
                      waits=(w if j == 0 else []),
                      out=ring[:, pi * 3072 + j * 1024: pi * 3072 + j * 1024 + 1024]
                      .rearrange("p (c j2) -> p c j2", c=8),
                      in_=wqkvT_h[:, colof:colof + 128]
                      .rearrange("(c p) j2 -> p c j2", p=128))
        qkv_loads(0)
        if NHP > 1:
            qkv_loads(1)
        for hp in range(NHP):
            pi = hp % 2
            rb = pi * 3072
            if KSUB < 2:
                continue
            pr.wait("tensor", s_gp, gpmark[hp] + 16)
            pr.wait("tensor", s_sc)
            for tt in range(NT):
                for c in range(NC8):
                    pr.op("tensor", "matmul", S_ps[:, tt * 512:(tt + 1) * 512],
                          ring[:, rb + c * 128: rb + (c + 1) * 128],
                          xnch(c, tt * TW, TW), start=(c == 0), stop=(c == 7),
                          inc=(s_mm if (tt == 3 and c == 7) else None))
            pr.wait("tensor", s_gp, gpmark[hp] + 32)
            for tt in range(NT):
                for c in range(NC8):
                    pr.op("tensor", "matmul", O_ps[:, tt * 512:(tt + 1) * 512],
                          ring[:, rb + 1024 + c * 128: rb + 1024 + c * 128 + 128],
                          xnch(c, tt * TW, TW), start=(c == 0), stop=(c == 7),
                          inc=(s_mm if (tt == 3 and c == 7) else None))
            pr.wait("scalar", s_mm, s_mm.n - 1)
            pr.op("scalar", "copy", out=qp_sb[:], in_=S_ps[:], inc=s_sc)
            pr.wait("scalar", s_mm)
            pr.op("scalar", "copy", out=kp_sb[:], in_=O_ps[:], inc=s_sc)
            if KSUB < 3:
                continue
            for j, src_sb in enumerate([qp_sb, kp_sb]):
                pr.wait("scalar", s_sc, s_sc.n - 1 + j)
                if j == 1:
                    pr.wait("scalar", s_ve)
                pr.op("scalar", "activation", out=pexp_sb[:], in_=src_sb[:],
                      func=AF.Square, inc=s_sc)
                pr.wait("tensor", s_sc)
                for b in range(4):
                    pr.op("tensor", "matmul", O_ps[0:1, b * 512:(b + 1) * 512],
                          onesc_bf[:], pexp_sb[:, b * 512:(b + 1) * 512],
                          start=True, stop=True,
                          inc=(s_mm if b == 3 else None))
                pr.wait("vector", s_mm)
                pr.op("vector", "tensor_reduce", out=sca[0:1, j:j + 1],
                      in_=O_ps[0:1, :], axis=AXX, op=ALU.max, inc=s_ve)
            pr.wait("vector", s_ve)
            pr.op("vector", "tensor_tensor", out=sca[0:1, 2:3], in0=sca[0:1, 0:1],
                  in1=sca[0:1, 1:2], op=ALU.add, inc=s_ve)
            pr.wait("vector", s_ve)
            pr.op("vector", "tensor_scalar", out=sca[0:1, 2:3], in0=sca[0:1, 2:3],
                  scalar1=1.0 / 16, scalar2=None, op0=ALU.mult, inc=s_ve)
            pr.wait("vector", s_ve)
            pr.op("vector", "tensor_scalar", out=sca[0:1, 3:4], in0=sca[0:1, 2:3],
                  scalar1=-1.0, scalar2=None, op0=ALU.mult, inc=s_ve)
            pr.wait("scalar", s_ve)
            pr.op("scalar", "activation", out=sca[0:1, 4:5], in_=sca[0:1, 2:3],
                  func=AF.Exp, inc=s_sc)
            if KSUB < 4:
                continue
            pr.wait("tensor", s_sc)
            cc = hp * 2
            pr.op("tensor", "matmul", O_ps[0:128, cc:cc + 1], onesr_f[:],
                  sca[0:1, 3:4], start=True, stop=True)
            pr.op("tensor", "matmul", O_ps[0:128, cc + 1:cc + 2], onesr_f[:],
                  sca[0:1, 4:5], start=True, stop=True, inc=s_mm)
            pr.wait("vector", s_mm)
            pr.op("vector", "tensor_copy", out=c0neg[:],
                  in_=O_ps[0:128, cc:cc + 1])
            pr.op("vector", "tensor_copy", out=ecol[:],
                  in_=O_ps[0:128, cc + 1:cc + 2], inc=s_ve)
            pr.wait("vector", s_ve)

            if KSUB < 5:
                continue
            vev0 = s_ve.n
            for Q in range(4):
                bank = SB(0) if Q % 2 == 0 else SB(1)
                w = [(s_ve, vev0)] if Q < 2 else [(s_ve, vev0 + (Q - 1) * 4)]
                for c in range(NC8):
                    for i in range(4):
                        tc = 4 * Q + i
                        pr.op("tensor", "matmul",
                              bank[0:128, i * 128:(i + 1) * 128],
                              xn_sb[:, c * T + tc * 128: c * T + (tc + 1) * 128],
                              ring[:, rb + 2048 + c * 128: rb + 2048 + c * 128 + 128],
                              start=(c == 0), stop=(c == 7),
                              waits=(w if (c == 0 and i == 0) else []),
                              inc=(s_mm if (c == 7 and i == 3) else None))
                pr.wait("vector", s_mm)
                for i in range(4):
                    tc = 4 * Q + i
                    base = tc * 144
                    pr.op("vector", "tensor_scalar",
                          out=vp_sb[:, base:base + 64],
                          in0=bank[0:128, i * 128:i * 128 + 64],
                          scalar1=ecol[:], scalar2=None, op0=ALU.mult)
                    pr.op("vector", "tensor_scalar",
                          out=vp_sb[:, base + 72:base + 136],
                          in0=bank[0:128, i * 128 + 64:i * 128 + 128],
                          scalar1=ecol[:], scalar2=None, op0=ALU.mult)
                    pr.op("vector", "tensor_scalar",
                          out=vp_sb[:, base + 64:base + 65], in0=onesc_bf[:],
                          scalar1=ecol[:], scalar2=None, op0=ALU.mult)
                    pr.op("vector", "tensor_scalar",
                          out=vp_sb[:, base + 136:base + 137], in0=onesc_bf[:],
                          scalar1=ecol[:], scalar2=None, op0=ALU.mult,
                          inc=s_ve)
            mm_pair_end[hp] = s_mm.n
            if hp + 2 < NHP:
                qkv_loads(hp + 2)

            prev_exp_last = None
            for hi in range(NHI):
                qs = slice(64 * hi, 64 * hi + 64)
                h = 2 * hp + hi
                pbuf = [pexp_sb, h2_sb]
                exp_mark = {}
                av_mark = {}
                mask_mark = {}
                ve_ent = s_ve.n
                sl_ent = s_l.n
                sc_ent = s_sc.n
                for kt in range(NKT):
                    qd = kt // 4
                    pb = pbuf[kt % 2]
                    for qtile in range(qd, 4):
                        wv = []
                        if (kt - 1, qtile) in exp_mark:
                            wv.append((s_sc, exp_mark[(kt - 1, qtile)]))
                        elif kt == 0:
                            if prev_exp_last is not None:
                                wv.append((s_sc, prev_exp_last[qtile]))
                            else:
                                wv.append((s_sc, sc_ent))
                            if hi == 0:
                                wv.append((s_ve, ve_ent))
                        pr.op("tensor", "matmul",
                              S_ps[:, qtile * 512:(qtile + 1) * 512],
                              kp_sb[qs, kt * 128:(kt + 1) * 128],
                              qp_sb[qs, qtile * 512:(qtile + 1) * 512],
                              start=True, stop=True, waits=wv, inc=s_mm)
                        mmv = s_mm.n
                        wexp = [(s_mm, mmv)]
                        if (kt - 2, qtile) in av_mark:
                            wexp.append((s_mm, av_mark[(kt - 2, qtile)]))
                        if kt == 0:
                            wexp += [(s_ve, ve_ent), (s_l, sl_ent)]
                        pr.op("scalar", "activation",
                              out=pb[:, qtile * 512:(qtile + 1) * 512],
                              in_=S_ps[:, qtile * 512:(qtile + 1) * 512],
                              func=AF.Exp, scale=0.125,
                              bias=c0neg[:], waits=wexp, inc=s_sc)
                        exp_mark[(kt, qtile)] = s_sc.n
                    r = kt * 128 - qd * 512
                    pr.op("vector", "tensor_tensor",
                          out=pb[:, qd * 512:qd * 512 + 512],
                          in0=pb[:, qd * 512:qd * 512 + 512],
                          in1=band_t[:, 384 - r:384 - r + 512], op=ALU.mult,
                          waits=[(s_sc, exp_mark[(kt, qd)])], inc=s_ve)
                    mask_mark[kt] = s_ve.n
                    for qtile in range(qd, 4):
                        if qtile == qd:
                            wav = [(s_ve, mask_mark[kt])]
                        else:
                            wav = [(s_sc, exp_mark[(kt, qtile)])]
                        pr.op("tensor", "matmul",
                              O_ps[0:65, qtile * 512:(qtile + 1) * 512],
                              vp_sb[:, kt * 144 + 72 * hi:
                                    kt * 144 + 72 * hi + 65],
                              pb[:, qtile * 512:(qtile + 1) * 512],
                              start=(kt == 0), stop=(kt == 4 * qtile + 3),
                              waits=wav, inc=s_mm)
                        av_mark[(kt, qtile)] = s_mm.n
                if True:
                    prev_exp_last = {q: exp_mark[(4 * q + 3, q)] for q in range(4)}
                pr.wait("scalar", s_mm)
                pr.op("scalar", "copy", out=pexp_sb[0:65, :], in_=O_ps[0:65, :],
                      inc=s_sc)
                pr.wait("gpsimd", s_sc)
                pr.op("gpsimd", "dma_start", inc=s_l, inck=16,
                      out=rowsB[h:h + 1, :], in_=pexp_sb[64:65, :])
                pr.wait("vector", s_sc)
                for tt in range(NT):
                    pr.op("vector", "tensor_copy",
                          out=yo_sb[64 * hi:64 * hi + 64,
                                    hp * T + tt * 512: hp * T + tt * 512 + 512],
                          in_=pexp_sb[0:64, tt * 512:(tt + 1) * 512],
                          inc=(s_ve if tt == 3 else None))
            barrier()
        tap_store("c0neg", c0neg[:])
        tap_store("ecoltap", ecol[:])
        tap_store("qp7", qp_sb[:])
        tap_store("kp7", kp_sb[:])
        tap_store("vp7", vp_sb[:])
        tap_store("ouns", yo_sb[:])
        tap_store("lall", rowsB[0:16, :])

        if lvl < 4:
            return
        pr.wait("vector", s_l)
        pr.op("vector", "reciprocal", out=rowsB[0:16, :], in_=rowsB[0:16, :], inc=s_ve)
        pr.wait("scalar", s_ve)
        pr.op("scalar", "copy", out=pexp_sb[0:16, :], in_=rowsB[0:16, :], inc=s_sc)
        lstage = [pexp_sb[64:65, :], h2_sb[64:65, 0:T]]
        sc_cast = s_sc.n
        mm_base_l = s_mm.n
        gp_l0 = s_gp.n
        pr.op("gpsimd", "dma_start", inc=s_gp, inck=16,
              waits=[(s_sc, sc_cast), (s_mm, mm_base_l)],
              out=lstage[0], in_=pexp_sb[0:1, :])
        pr.op("gpsimd", "dma_start", inc=s_gp, inck=16,
              out=lstage[1], in_=pexp_sb[1:2, :])
        mm_lrep = {}
        for h in range(H):
            hp, hi = h // 2, h % 2
            pr.wait("tensor", s_gp, gp_l0 + 16 * (h + 1))
            pr.wait("tensor", s_ve)
            st_l = lstage[h % 2]
            for tt in range(NT):
                pr.op("tensor", "matmul", SB(tt)[0:64, :], onesr_b3[64:65, 0:64],
                      st_l[0:1, tt * 512:(tt + 1) * 512], start=True, stop=True,
                      inc=(s_mm if tt == 3 else None))
            mm_lrep[h] = s_mm.n
            if h + 2 < H:
                pr.op("gpsimd", "dma_start", inc=s_gp, inck=16,
                      waits=[(s_mm, mm_lrep[h])],
                      out=lstage[h % 2], in_=pexp_sb[h + 2:h + 3, :])
            pr.wait("vector", s_mm)
            for tt in range(NT):
                dst = yo_sb[64 * hi:64 * hi + 64,
                            hp * T + tt * 512: hp * T + tt * 512 + 512]
                pr.op("vector", "tensor_tensor", out=dst, in0=dst,
                      in1=SB(tt)[0:64, :], op=ALU.mult,
                      inc=(s_ve if tt == 3 else None))
        barrier()
        tap_store("onrm", yo_sb[:])

        if lvl < 5:
            return
        for o2c in range(8):
            gpw = [] if o2c < 2 else [(s_ve, s_ve.n)]
            gpld = s_gp.n
            pr.op("gpsimd", "dma_start", inc=s_gp, inck=16, waits=gpw,
                  out=ring[:, (o2c % 2) * 2048: (o2c % 2) * 2048 + 1024]
                  .rearrange("p (c j) -> p c j", c=8),
                  in_=woutT_h[:, o2c * 128:(o2c + 1) * 128]
                  .rearrange("(c p) j -> p c j", p=128))
            pr.wait("tensor", s_gp, gpld + 16)
            for tt in range(NT):
                bank = OB(tt % 2)
                wbk = [(s_ve, s_ve.n)] if (tt >= 2 or o2c > 0) else []
                for c in range(NC8):
                    pr.op("tensor", "matmul", bank,
                          ring[:, (o2c % 2) * 2048 + c * 128:
                               (o2c % 2) * 2048 + c * 128 + 128],
                          yo_sb[:, c * T + tt * 512: c * T + tt * 512 + 512],
                          start=(c == 0), stop=(c == 7),
                          waits=(wbk if c == 0 else []),
                          inc=(s_mm if c == 7 else None))
                sslot = stg if tt % 2 == 0 else stg2
                wv_ = [(s_mm, s_mm.n)]
                if tt >= 2 or o2c > 0:
                    wv_.append((s_gp, s_gp.n - 1))
                pr.op("vector", "tensor_scalar", out=sslot[:], in0=bank,
                      scalar1=cols[:, COL["bout"] + o2c:COL["bout"] + o2c + 1],
                      scalar2=None, op0=ALU.add, waits=wv_, inc=s_ve)
                pr.op("gpsimd", "tensor_tensor", out=xch(o2c, tt * TW, TW),
                      in0=xch(o2c, tt * TW, TW), in1=sslot[:], op=ALU.add,
                      waits=[(s_ve, s_ve.n)], inc=s_gp)
        barrier()
        tap_store("x1", x_sb[:])

    def emit_L1(s, scanA, scanB, sqbf):
        if lvl < 6:
            return
        emit_stats(sqbf)
        barrier()
        for tt in range(NT):
            t0 = tt * TW
            tsl = slice(t0, t0 + TW)
            pr.op("tensor", "matmul", SB(0), onesr_f[:], RA(0, tsl),
                  start=True, stop=True)
            pr.op("tensor", "matmul", SB(1), onesr3[32:33, 0:128], RA(32, tsl),
                  start=True, stop=True, inc=s_mm)
            pr.wait("vector", s_mm)
            for c in range(NC8):
                pr.op("vector", "tensor_tensor", out=stg[:], in0=xch(c, t0, TW),
                      in1=SB(0), op=ALU.subtract)
                pr.op("vector", "tensor_tensor", out=xnch(c, t0, TW), in0=stg[:],
                      in1=SB(1), op=ALU.mult,
                      inc=(s_ve if c == NC8 - 1 else None))
            pr.wait("tensor", s_ve)
        barrier()
        tap_store("xn1", xn_sb[:])

        if lvl < 7:
            return
        pr.op("gpsimd", "dma_start", inc=s_gp, inck=16,
              out=ring[:, 4096:4480].rearrange("p (c j) -> p c j", c=8),
              in_=win_h[:, 2 * D: 2 * D + 48].rearrange("(c p) j -> p c j", p=128))
        pr.wait("tensor", s_gp)
        sdst = [scanA[0:16, :], scanB[0:16, :], rowsA[0:16, :]]
        for j, dst in enumerate(sdst):
            wbk = [(s_ve, s_ve.n)] if j > 0 else []
            for tt in range(NT):
                for c in range(NC8):
                    pr.op("tensor", "matmul",
                          S_ps[0:16, tt * 512:(tt + 1) * 512],
                          ring[:, 4096 + c * 48 + 16 * j: 4096 + c * 48 + 16 * j + 16],
                          xnch(c, tt * TW, TW), start=(c == 0), stop=(c == 7),
                          waits=(wbk if (tt == 0 and c == 0) else []),
                          inc=(s_mm if (tt == 3 and c == 7) else None))
            pr.wait("vector", s_mm)
            pr.op("vector", "tensor_scalar", out=dst,
                  in0=S_ps[0:16, :], scalar1=sc16[0:16, j:j + 1],
                  scalar2=None, op0=ALU.add, inc=s_ve)
        bcd_ve = s_ve.n
        dl = rowsA[0:16, :]
        tmp = rowsB[0:16, :]
        pr.op("scalar", "activation", out=tmp, in_=dl, func=AF.Abs,
              waits=[(s_ve, bcd_ve)])
        pr.op("scalar", "activation", out=tmp, in_=tmp, func=AF.Exp, scale=-1.0)
        pr.op("scalar", "activation", out=tmp, in_=tmp, func=AF.Ln, bias=1.0,
              inc=s_sc)
        pr.wait("vector", s_sc)
        pr.op("vector", "tensor_scalar", out=dl, in0=dl,
              scalar1=0.0, scalar2=None, op0=ALU.max, inc=s_ve)
        pr.wait("vector", s_ve)
        pr.op("vector", "tensor_tensor", out=dl, in0=dl, in1=tmp, op=ALU.add,
              inc=s_ve)
        pr.wait("vector", s_ve)
        pr.op("vector", "tensor_scalar", out=tmp, in0=dl,
              scalar1=sc16[0:16, 4:5], scalar2=None, op0=ALU.mult, inc=s_ve)
        pr.wait("scalar", s_ve)
        pr.op("scalar", "activation", out=tmp, in_=tmp, func=AF.Exp, inc=s_sc)
        pr.wait("vector", s_sc)
        pr.op("vector", "tensor_tensor_scan", out=dl,
              data0=tmp, data1=scanA[0:16, :], initial=0.0,
              op0=ALU.mult, op1=ALU.add, inc=s_ve)
        pr.wait("vector", s_ve)
        pr.op("vector", "tensor_tensor", out=scanB[0:16, :],
              in0=scanB[0:16, :], in1=dl, op=ALU.mult, inc=s_ve)
        scan_ve = s_ve.n

        for pt in range(8):
            gpw = [] if pt < 2 else [(s_sc, s_sc.n)]
            gpld = s_gp.n
            pr.op("gpsimd", "dma_start", inc=s_gp, inck=16, waits=gpw,
                  out=ring[:, (pt % 2) * 2048:(pt % 2) * 2048 + 1024]
                  .rearrange("p (c j) -> p c j", c=8),
                  in_=win_h[:, D + pt * 128: D + (pt + 1) * 128]
                  .rearrange("(c p) j -> p c j", p=128))
            pr.wait("tensor", s_gp, gpld + 16)
            for tt in range(NT):
                bank = OB(tt % 2)
                wbk = [(s_sc, s_sc.n)] if tt >= 2 or pt > 0 else []
                for c in range(NC8):
                    pr.op("tensor", "matmul", bank,
                          ring[:, (pt % 2) * 2048 + c * 128:
                               (pt % 2) * 2048 + c * 128 + 128],
                          xnch(c, tt * TW, TW), start=(c == 0), stop=(c == 7),
                          waits=(wbk if c == 0 else []),
                          inc=(s_mm if c == 7 else None))
                pr.wait("scalar", s_mm)
                pr.op("scalar", "activation",
                      out=yo_sb[:, pt * T + tt * 512: pt * T + tt * 512 + 512],
                      in_=bank, func=AF.Gelu,
                      bias=cols[:, COL["binz"] + pt:COL["binz"] + pt + 1],
                      inc=s_sc)
        gate_sc = s_sc.n
        pr.wait("tensor", s_ve, scan_ve)
        for b in range(4):
            pr.op("tensor", "matmul", S_ps[0:1, b * 512:(b + 1) * 512],
                  sc16[0:16, 5:6], scanB[0:16, b * 512:(b + 1) * 512],
                  start=True, stop=True, inc=(s_mm if b == 3 else None))
        pr.wait("vector", s_mm)
        pr.op("vector", "tensor_copy", out=RB(32, SL), in_=S_ps[0:1, :],
              inc=s_ve)
        tap_store("scanS", rowsA[0:16, :])
        tap_store("yrow", rowsB[32:33, :])

        if lvl < 9:
            return
        pr.wait("vector", s_sc, gate_sc)
        pr.wait("tensor", s_ve)
        tap_store("gtap", yo_sb[:])
        for tt in range(NT):
            pr.op("tensor", "matmul", SB(0), onesr3[32:33, 0:128],
                  RB(32, slice(tt * 512, tt * 512 + 512)), start=True,
                  stop=True, inc=s_mm)
            pr.wait("vector", s_mm)
            for c in range(NC8):
                dst = yo_sb[:, c * T + tt * 512: c * T + tt * 512 + 512]
                pr.op("vector", "tensor_tensor", out=dst, in0=dst, in1=SB(0),
                      op=ALU.mult, inc=(s_ve if c == NC8 - 1 else None))
            pr.wait("tensor", s_ve)
        barrier()
        tap_store("utap", yo_sb[:])

        if lvl < 10:
            return
        for o2c in range(8):
            gpw = [] if o2c < 2 else [(s_ve, s_ve.n)]
            gpld = s_gp.n
            pr.op("gpsimd", "dma_start", inc=s_gp, inck=16, waits=gpw,
                  out=ring[:, (o2c % 2) * 2048:(o2c % 2) * 2048 + 1024]
                  .rearrange("p (c j) -> p c j", c=8),
                  in_=wout1_h[:, o2c * 128:(o2c + 1) * 128]
                  .rearrange("(c p) j -> p c j", p=128))
            pr.wait("tensor", s_gp, gpld + 16)
            for tt in range(NT):
                bank = OB(tt % 2)
                wbk = [(s_ve, s_ve.n)] if (tt >= 2 or o2c > 0) else []
                for c in range(NC8):
                    pr.op("tensor", "matmul", bank,
                          ring[:, (o2c % 2) * 2048 + c * 128:
                               (o2c % 2) * 2048 + c * 128 + 128],
                          yo_sb[:, c * T + tt * 512: c * T + tt * 512 + 512],
                          start=(c == 0), stop=False,
                          waits=(wbk if c == 0 else []))
                for c in range(NC8):
                    pr.op("tensor", "matmul", bank,
                          wbar_sb[:, c * D + o2c * 128: c * D + (o2c + 1) * 128],
                          xnch(c, tt * TW, TW), start=False, stop=(c == 7),
                          inc=(s_mm if c == 7 else None))
                sslot = stg if tt % 2 == 0 else stg2
                wv_ = [(s_mm, s_mm.n)]
                if tt >= 2 or o2c > 0:
                    wv_.append((s_gp, s_gp.n - 1))
                pr.op("vector", "tensor_scalar", out=sslot[:], in0=bank,
                      scalar1=cols[:, COL["badd"] + o2c:COL["badd"] + o2c + 1],
                      scalar2=None, op0=ALU.add, waits=wv_, inc=s_ve)
                pr.op("gpsimd", "tensor_tensor", out=xch(o2c, tt * TW, TW),
                      in0=xch(o2c, tt * TW, TW), in1=sslot[:], op=ALU.add,
                      waits=[(s_ve, s_ve.n)], inc=s_gp)
        barrier()

    for s in range(NSEQ):
        if do_l0:
            stA = contextlib.ExitStack()
            qp_sb = stA.enter_context(nc.sbuf_tensor(f"qp{s}", [128, T], BF16))
            kp_sb = stA.enter_context(nc.sbuf_tensor(f"kp{s}", [128, T], BF16))
            vp_sb = stA.enter_context(nc.sbuf_tensor(f"vp{s}", [128, 2304], BF16))
            h2_sb = stA.enter_context(
                nc.sbuf_tensor(f"h2{s}", [128, NC8 * TW], BF16))
            emit_L0(s, qp_sb, kp_sb, vp_sb, h2_sb)
            stA.close()
        else:
            if s > 0:
                pr.op("sync", "dma_start", inc=s_ld, inck=16,
                      waits=[(s_st, s_st.n)],
                      out=x_sb[:].rearrange("p (c t) -> p c t", c=NC8),
                      in_=x_h[s].rearrange("(c p) t -> p c t", p=128))
            barrier()
        if do_l1:
            stB = contextlib.ExitStack()
            scanA = stB.enter_context(nc.sbuf_tensor(f"scA{s}", [128, T], F32))
            scanB = stB.enter_context(nc.sbuf_tensor(f"scB{s}", [128, T], F32))
            sqbf = stB.enter_context(nc.sbuf_tensor(f"sq{s}", [128, T], BF16))
            emit_L1(s, scanA, scanB, sqbf)
            stB.close()
        barrier()
        pr.op("sync", "dma_start", inc=s_st, inck=16,
              out=out_h[s].rearrange("(c p) t -> p c t", p=128),
              in_=x_sb[:].rearrange("p (c t) -> p c t", c=NC8))
    pr.wait("sync", s_st)

    with nc.Block() as block:
        for ename in Prog.ENGS:
            fns = pr.items[ename]
            if not fns:
                continue

            def body(eng, fns=fns):
                for fn in fns:
                    fn(eng)
            getattr(block, ename)(body)

    stk.close()
    return nc, tap_hs


def _mk_smallrow(inp):
    sr = np.zeros((1, SR_LEN), np.float32)
    vals = {"bout": inp["L0_attn_bout"], "gamma": inp["L0_ffn_ln_gamma"],
            "beta": inp["L0_ffn_ln_beta"], "fc1b": inp["L0_ffn_fc1_bias"],
            "fc2b": inp["L0_ffn_fc2_bias"],
            "binz": inp["L1_attn_b_in"][D:2 * D],
            "binB": inp["L1_attn_b_in"][2 * D:2 * D + 16],
            "binC": inp["L1_attn_b_in"][2 * D + 16:2 * D + 32],
            "binD": inp["L1_attn_b_in"][2 * D + 32:2 * D + 48],
            "alog": inp["L1_attn_A_log"], "bout1": inp["L1_attn_b_out"]}
    for name, v in vals.items():
        off, ln = _SR[name]
        sr[0, off:off + ln] = np.asarray(v, np.float32)
    return sr


def _consts():
    band = np.zeros((128, 896), np.float32)
    for i in range(128):
        band[i, i + 384:] = 1.0
    return {"i128f": np.eye(128, dtype=np.float32),
            "bandbf": band.astype(ml_dtypes.bfloat16)}


_FUSED = True
_TAPS = ()


def _host_inputs(inp):
    inp = {k: np.asarray(v) for k, v in inp.items()}
    x = np.ascontiguousarray(inp["L0_attn_x"].astype(np.float32))
    common = dict(smallrow=_mk_smallrow(inp), **_consts())
    l0w = dict(
        wqkvT=np.ascontiguousarray(inp["L0_attn_Wqkv"].T.astype(np.float32)),
        woutT=np.ascontiguousarray(inp["L0_attn_Wout"].T.astype(np.float32)),
        fc1q=np.ascontiguousarray(inp["L0_ffn_fc1_q"].astype(np.uint8)),
        fc2q=np.ascontiguousarray(inp["L0_ffn_fc2_q"].astype(np.uint8)),
        fc1s=np.ascontiguousarray(inp["L0_ffn_fc1_scales"].astype(np.float32)),
        fc2s=np.ascontiguousarray(inp["L0_ffn_fc2_scales"].astype(np.float32)))
    l1w = dict(
        win=np.ascontiguousarray(inp["L1_attn_W_in"].astype(np.float32)),
        wout1=np.ascontiguousarray(inp["L1_attn_W_out"].astype(np.float32)),
        l1w=np.ascontiguousarray(
            inp["L1_ffn_W"].astype(np.float32).reshape(32 * D, D)),
        l1bf=np.ascontiguousarray(
            inp["L1_ffn_b"].astype(np.float32).reshape(1, 32 * D)),
        ids=np.ascontiguousarray(inp["L1_ffn_ids"].astype(np.int32)),
        wts=np.ascontiguousarray(
            inp["L1_ffn_weights"].astype(np.float32).reshape(1, 8)))
    shards = [np.ascontiguousarray(x[2 * c:2 * c + 2].transpose(0, 2, 1))
              for c in range(NCORES)]
    return shards, common, l0w, l1w


def kernel(**inp):
    shards, common, l0w, l1w = _host_inputs(inp)
    if _FUSED:
        nc, _ = build(("L0", "L1"), taps=_TAPS)
        in_maps = [dict(xT=shards[c], **common, **l0w, **l1w)
                   for c in range(NCORES)]
        res = run_bass_kernel_spmd(nc, in_maps, core_ids=list(range(NCORES)))
        outs = [res.results[c]["outT"] for c in range(NCORES)]
    else:
        ncA, _ = build(("L0",), taps=_TAPS)
        in_maps = [dict(xT=shards[c], **common, **l0w) for c in range(NCORES)]
        resA = run_bass_kernel_spmd(ncA, in_maps, core_ids=list(range(NCORES)))
        x1s = [np.asarray(resA.results[c]["outT"]) for c in range(NCORES)]
        ncB, _ = build(("L1",), taps=_TAPS)
        in_mapsB = [dict(xT=np.ascontiguousarray(x1s[c]), **common, **l1w)
                    for c in range(NCORES)]
        resB = run_bass_kernel_spmd(ncB, in_mapsB, core_ids=list(range(NCORES)))
        outs = [resB.results[c]["outT"] for c in range(NCORES)]
    out = np.empty((16, T, D), np.float32)
    for c in range(NCORES):
        out[2 * c:2 * c + 2] = np.asarray(outs[c], np.float32).transpose(0, 2, 1)
    return out


# version 14 (final)
# speedup vs baseline: 1.1858x; 1.0067x over previous
import contextlib
import numpy as np
import ml_dtypes
import concourse.bass as bass
import concourse.mybir as mybir
from concourse.bass_utils import run_bass_kernel_spmd

F32 = mybir.dt.float32
BF16 = mybir.dt.bfloat16
I32 = mybir.dt.int32
AF = mybir.ActivationFunctionType
ALU = mybir.AluOpType
AXX = mybir.AxisListType.X

NCORES = 8
NSEQ = 2
D = 1024
T = 2048
H = 16
F4 = 4096
NT = 4
TW = 512
NC8 = 8
EPS = 1e-5

_SR = {}
def _mk_sr():
    off = 0
    for name, ln in [("bout", D), ("gamma", D), ("beta", D), ("fc1b", F4),
                     ("fc2b", D), ("binz", D), ("binB", 16), ("binC", 16),
                     ("binD", 16), ("alog", 16), ("bout1", D)]:
        if ln >= 128:
            off = ((off + 127) // 128) * 128
        _SR[name] = (off, ln)
        off += ln
    return ((off + 2047) // 2048) * 2048
SR_LEN = _mk_sr()


class Prog:
    ENGS = ("sync", "gpsimd", "scalar", "vector", "tensor")

    def __init__(self):
        self.items = {e: [] for e in self.ENGS}

    def raw(self, eng, fn):
        self.items[eng].append(fn)

    def op(self, eng, method, *args, inc=None, inck=1, waits=(), **kw):
        waits = [(c.h, v) for (c, v) in waits]
        inch = None
        if inc is not None:
            inc.n += inck
            inch = inc.h

        def fn(e, waits=waits, method=method, args=args, kw=kw, inch=inch,
               inck=inck):
            for h, v in waits:
                e.wait_ge(h, v)
            instr = getattr(e, method)(*args, **kw)
            if inch is not None:
                instr.then_inc(inch, inck)
            return instr
        self.items[eng].append(fn)

    def wait(self, eng, ctr, val=None):
        v = ctr.n if val is None else val
        self.items[eng].append(lambda e, h=ctr.h, v=v: e.wait_ge(h, v))


class Ctr:
    def __init__(self, stk, nc, name):
        self.h = stk.enter_context(nc.semaphore(name))
        self.n = 0


def build(phases=("L0", "L1"), x_in_name="xT", out_name="outT", taps=(), lvl=99):
    nc = bass.Bass("TRN2", target_bir_lowering=False, debug=False)
    stk = contextlib.ExitStack()
    taps = set(taps)
    do_l0 = "L0" in phases
    do_l1 = "L1" in phases
    pr = Prog()

    def din(name, shape, dt=F32):
        return nc.dram_tensor(name, shape, dt, kind="ExternalInput").ap()

    x_h = din(x_in_name, [NSEQ, D, T])
    out_h = nc.dram_tensor(out_name, [NSEQ, D, T], F32, kind="ExternalOutput").ap()
    sr_h = din("smallrow", [1, SR_LEN])
    i128f_h = din("i128f", [128, 128])
    band_h = din("bandbf", [128, 896], BF16)
    if do_l0:
        wqkvT_h = din("wqkvT", [D, 3 * D])
        woutT_h = din("woutT", [D, D])
        fc1q_h = din("fc1q", [D, F4], mybir.dt.uint8)
        fc2q_h = din("fc2q", [F4, D], mybir.dt.uint8)
        fc1s_h = din("fc1s", [8, F4])
        fc2s_h = din("fc2s", [32, D])
        w1s = nc.dram_tensor("w1s", [128, 32, 2048], BF16, kind="Internal").ap()
        w2s = nc.dram_tensor("w2s", [F4, D], BF16, kind="Internal").ap()
    if do_l1:
        win_h = din("win", [D, 2096])
        wout1_h = din("wout1", [D, D])
        l1w_h = din("l1w", [32 * D, D])
        l1bf_h = din("l1bf", [1, 32 * D])
        ids_h = din("ids", [8], I32)
        wts_h = din("wts", [1, 8])
        wsel = nc.dram_tensor("wsel", [8 * D, D], F32, kind="Internal").ap()

    tap_hs = {}

    s_ld = Ctr(stk, nc, "s_ld")
    s_gp = Ctr(stk, nc, "s_gp")
    s_mm = Ctr(stk, nc, "s_mm")
    s_sc = Ctr(stk, nc, "s_sc")
    s_ve = Ctr(stk, nc, "s_ve")
    s_st = Ctr(stk, nc, "s_st")
    s_l = Ctr(stk, nc, "s_l")
    s_id = Ctr(stk, nc, "s_id")
    ALLC = [s_ld, s_gp, s_mm, s_sc, s_ve]

    def barrier():
        for e in Prog.ENGS:
            for c in ALLC:
                pr.wait(e, c)

    def tap_store(name, ap):
        if name not in taps or name in tap_hs:
            return
        barrier()
        tap_hs[name] = nc.dram_tensor("tap_" + name, list(ap.shape), ap.dtype,
                                      kind="ExternalOutput").ap()
        pr.op("sync", "dma_start", inc=s_st, inck=16,
              out=tap_hs[name][:, :], in_=ap)
        barrier()

    def tap_dram(name, dap):
        if name not in taps or name in tap_hs:
            return
        barrier()
        tap_hs[name] = nc.dram_tensor("tap_" + name, list(dap.shape), dap.dtype,
                                      kind="ExternalOutput").ap()
        pr.op("gpsimd", "dma_start", inc=s_gp, inck=16,
              out=tap_hs[name][:, :], in_=dap)
        barrier()

    sbP = lambda n, sh, dt: stk.enter_context(nc.sbuf_tensor(n, sh, dt))
    cols = sbP("cols", [128, 80], F32)
    COL = {"bout": 0, "gamma": 8, "beta": 16, "fc1b": 24, "fc2b5": 56,
           "binz": 64, "badd": 72}
    i128f = sbP("i128f_t", [128, 128], F32)
    band_t = sbP("band_t", [128, 896], BF16)
    onesc_bf = sbP("onesc_bf", [128, 1], BF16)
    onesr_f = sbP("onesr_f", [1, 128], F32)
    onesr3 = sbP("onesr3", [65, 128], F32)
    halfr = sbP("halfr", [65, 128], F32)
    onesr_b3 = sbP("onesr_b3", [65, 128], BF16)
    sca = sbP("sca", [1, 16], F32)
    epsc = sbP("epsc", [1, 2], F32)
    c0neg = sbP("c0neg", [128, 1], F32)
    ecol = sbP("ecol", [128, 1], F32)
    sc16 = sbP("sc16", [16, 8], F32)
    if do_l1:
        wbar_sb = sbP("wbar_sb", [128, 8 * D], BF16)
        wtsc_sb = sbP("wtsc", [8, 1], F32)
        wrep_sb = sbP("wrep", [128, 8], F32)
        ids_sb = sbP("ids_sb", [1, 8], I32)

    S_ps = stk.enter_context(nc.psum_tensor("S_ps", [128, T], F32))
    O_ps = stk.enter_context(nc.psum_tensor("O_ps", [128, T], F32))
    SB = lambda b: S_ps[:, b * 512:(b + 1) * 512]
    OB = lambda b: O_ps[:, b * 512:(b + 1) * 512]

    x_sb = stk.enter_context(nc.sbuf_tensor("x_sb", [128, NC8 * T], F32))
    rowsA = stk.enter_context(nc.sbuf_tensor("rowsA", [128, T], F32))
    rowsB = stk.enter_context(nc.sbuf_tensor("rowsB", [128, T], F32))
    pexp_sb = stk.enter_context(nc.sbuf_tensor("pexp_sb", [128, T], BF16))
    stg = stk.enter_context(nc.sbuf_tensor("stg", [128, TW], F32))
    stg2 = stk.enter_context(nc.sbuf_tensor("stg2", [128, TW], F32))

    stkS = contextlib.ExitStack()
    sbS = lambda n, sh, dt: stkS.enter_context(nc.sbuf_tensor(n, sh, dt))
    bbrow = sbS("bbrow", [1, 1024], F32)
    if do_l1:
        bsel_sb = sbS("bsel_sb", [8, D], F32)
        wkI_sb = sbS("wkI_sb", [128, 8 * 128], F32)
        ringS = sbS("ringS", [128, 6 * D], F32)
    if do_l0:
        qint = sbS("qint", [128, 2 * 2048], mybir.dt.uint8)
        qbf = sbS("qbf", [128, 2 * 2048], BF16)
        wtmp = sbS("wtmp", [128, 2 * 2048], BF16)

    pr.op("sync", "dma_start", inc=s_ld, inck=16, out=i128f[:], in_=i128f_h[:])
    pr.op("sync", "dma_start", inc=s_ld, inck=16, out=band_t[:], in_=band_h[:])
    _SLOT = [("A", 0), ("A", 32), ("A", 64), ("B", 0), ("B", 32), ("B", 64)]

    def srow_ap(off, ln):
        sl = off // 2048
        kind, r = _SLOT[sl]
        tns = rowsA if kind == "A" else rowsB
        o = off % 2048
        return tns[r:r + 1, o:o + ln], i128f[r:r + 1, r:r + 1]
    for sl in range(6):
        kind, r = _SLOT[sl]
        tns = rowsA if kind == "A" else rowsB
        w = min(2048, SR_LEN - sl * 2048)
        pr.op("sync", "dma_start", inc=s_ld, inck=16,
              out=tns[r:r + 1, 0:w], in_=sr_h[0:1, sl * 2048: sl * 2048 + w])
    pr.op("sync", "dma_start", inc=s_ld, inck=16,
          out=x_sb[:].rearrange("p (c t) -> p c t", c=NC8),
          in_=x_h[0].rearrange("(c p) t -> p c t", p=128))
    xld0_done = s_ld.n
    pr.wait("vector", s_ld)
    pr.op("vector", "memset", onesc_bf[:], 1.0)
    pr.op("vector", "memset", onesr_f[:], 1.0)
    pr.op("vector", "memset", onesr3[:], 1.0)
    pr.op("vector", "memset", halfr[:], 0.5)
    pr.op("vector", "memset", onesr_b3[:], 1.0)
    if do_l1:
        pr.raw("vector", lambda e: e.memset(sc16[0:16, 5:6], 1.0))
    pr.op("vector", "memset", sca[:], 0.0)
    pr.op("vector", "memset", epsc[0:1, 0:1], EPS)
    pr.op("vector", "memset", epsc[0:1, 1:2], EPS * EPS, inc=s_ve)

    pr.wait("tensor", s_ld)
    pr.wait("tensor", s_ve)
    colspec = []
    if do_l0:
        colspec += [("bout", COL["bout"], 8), ("gamma", COL["gamma"], 8),
                    ("beta", COL["beta"], 8), ("fc1b", COL["fc1b"], 32),
                    ("fc2b", COL["fc2b5"], 8)]
    if do_l1:
        colspec += [("binz", COL["binz"], 8), ("bout1", COL["badd"], 8)]
    flat = [(n, b, c) for (n, b, ncols) in colspec for c in range(ncols)]
    for i0 in range(0, len(flat), 16):
        grp = flat[i0:i0 + 16]
        for j, (name, base, c) in enumerate(grp):
            off, _ = _SR[name]
            sap, idap = srow_ap(off + c * 128, 128)
            pr.op("tensor", "transpose", O_ps[0:128, j:j + 1],
                  sap, idap, inc=(s_mm if j == len(grp) - 1 else None))
        pr.wait("vector", s_mm)
        for j, (name, base, c) in enumerate(grp):
            sc = 0.5 if name == "fc2b" else 1.0
            pr.op("vector", "tensor_scalar", out=cols[:, base + c:base + c + 1],
                  in0=O_ps[0:128, j:j + 1], scalar1=sc, scalar2=None,
                  op0=ALU.mult, inc=(s_ve if j == len(grp) - 1 else None))
        pr.wait("tensor", s_ve)
    if do_l1:
        for j, name in enumerate(["binB", "binC", "binD", "alog"]):
            off, _ = _SR[name]
            sap, idap = srow_ap(off, 16)
            pr.op("tensor", "transpose", O_ps[0:16, j:j + 1],
                  sap, idap, inc=(s_mm if j == 3 else None))
        pr.wait("vector", s_mm)
        pr.op("vector", "tensor_copy", out=sc16[0:16, 0:4], in_=O_ps[0:16, 0:4],
              inc=s_ve)
        pr.wait("scalar", s_ve)
        pr.op("scalar", "activation", out=sc16[0:16, 4:5], in_=sc16[0:16, 3:4],
              func=AF.Exp, inc=s_sc)
        pr.wait("vector", s_sc)
        pr.op("vector", "tensor_scalar", out=sc16[0:16, 4:5],
              in0=sc16[0:16, 4:5], scalar1=-1.0, scalar2=None, op0=ALU.mult,
              inc=s_ve)
        pr.wait("tensor", s_ve)

    if do_l1:
        pr.op("gpsimd", "dma_start", inc=s_gp, inck=16, out=ids_sb[:],
              in_=ids_h[:])
        pr.wait("gpsimd", s_gp)
        pr.op("gpsimd", "tensor_scalar", out=ids_sb[:], in0=ids_sb[:],
              scalar1=D, scalar2=None, op0=ALU.mult, inc=s_id)
        pr.wait("gpsimd", s_id)

        def emit_gather(e):
            _, vals = nc.values_load_multi_w_load_instructions(
                ids_sb[0:1, 0:8], engines=[mybir.EngineType.Pool],
                skip_runtime_bounds_check=True)
            for k in range(8):
                e.dma_start(out=wsel[k * D:(k + 1) * D, :],
                            in_=l1w_h[bass.ds(vals[k], D), :]).then_inc(s_gp.h, 16)
                e.dma_start(
                    out=bsel_sb[k:k + 1, :],
                    in_=l1bf_h[0:1, bass.ds(vals[k], D)]
                ).then_inc(s_gp.h, 16)
        gather_mark = {}
        g0 = s_gp.n
        for k in range(8):
            gather_mark[k] = g0 + 32 * (k + 1)
        pr.raw("gpsimd", emit_gather)
        s_gp.n += 16 * 16
        pr.op("gpsimd", "dma_start", inc=s_gp, inck=16, out=sca[0:1, 0:8],
              in_=wts_h[:], waits=[(s_ve, s_ve.n)])

        pr.wait("tensor", s_gp)
        pr.op("tensor", "transpose", O_ps[0:8, 16:17], sca[0:1, 0:8],
              i128f[0:1, 0:1])
        pr.op("tensor", "matmul", O_ps[0:128, 512:520], onesr_f[:],
              sca[0:1, 0:8], start=True, stop=True, inc=s_mm)
        pr.wait("vector", s_mm)
        pr.op("vector", "tensor_copy", out=wtsc_sb[:], in_=O_ps[0:8, 16:17])
        pr.op("vector", "tensor_copy", out=wrep_sb[:], in_=O_ps[0:128, 512:520],
              inc=s_ve)
        pr.wait("vector", s_ve)
        for k in range(8):
            pr.op("vector", "tensor_scalar", out=wkI_sb[:, k * 128:(k + 1) * 128],
                  in0=i128f[:], scalar1=wrep_sb[:, k:k + 1], scalar2=None,
                  op0=ALU.mult, inc=(s_ve if k == 7 else None))

        pr.wait("tensor", s_ve)
        pr.op("tensor", "matmul", O_ps[0:1, 1024:1536], wtsc_sb[:],
              bsel_sb[:, 0:512], start=True, stop=True)
        pr.op("tensor", "matmul", O_ps[0:1, 1536:2048], wtsc_sb[:],
              bsel_sb[:, 512:1024], start=True, stop=True, inc=s_mm)
        pr.wait("scalar", s_mm)
        pr.op("scalar", "copy", out=bbrow[0:1, 0:1024], in_=O_ps[0:1, 1024:2048],
              inc=s_sc)
        pr.wait("tensor", s_sc)
        for c in range(8):
            pr.op("tensor", "transpose", S_ps[0:128, c:c + 1],
                  bbrow[0:1, c * 128:(c + 1) * 128], i128f[0:1, 0:1],
                  inc=(s_mm if c == 7 else None))
        pr.wait("vector", s_mm)
        pr.op("vector", "tensor_tensor", out=cols[:, COL["badd"]:COL["badd"] + 8],
              in0=cols[:, COL["badd"]:COL["badd"] + 8], in1=S_ps[0:128, 0:8],
              op=ALU.add, inc=s_ve)

        barrier()
        RN = 6
        gp0, mm0, ve0 = s_gp.n, s_mm.n, s_ve.n
        for oc in range(8):
            for k in range(8):
                idx = oc * 8 + k
                slot = idx % RN
                w = [] if idx < RN else [(s_mm, mm0 + (idx - RN + 1))]
                w.append((s_gp, gather_mark[k]))
                pr.op("gpsimd", "dma_start", inc=s_gp, inck=16, waits=w,
                      out=ringS[:, slot * D:(slot + 1) * D],
                      in_=wsel[k * D + oc * 128: k * D + (oc + 1) * 128, :])
        for oc in range(8):
            for k in range(8):
                idx = oc * 8 + k
                slot = idx % RN
                w = [(s_gp, gp0 + 16 * (idx + 1))]
                if k == 0 and oc > 0:
                    w.append((s_ve, ve0 + oc))
                for dc in range(8):
                    pr.op("tensor", "matmul", S_ps[:, dc * 128:(dc + 1) * 128],
                          ringS[:, slot * D + dc * 128: slot * D + (dc + 1) * 128],
                          wkI_sb[:, k * 128:(k + 1) * 128],
                          start=(k == 0), stop=(k == 7),
                          waits=(w if dc == 0 else []),
                          inc=(s_mm if dc == 7 else None))
            pr.wait("vector", s_mm, mm0 + (oc + 1) * 8)
            for dc in range(8):
                pr.op("vector", "tensor_copy",
                      out=wbar_sb[:, dc * D + oc * 128: dc * D + (oc + 1) * 128],
                      in_=S_ps[:, dc * 128:(dc + 1) * 128],
                      inc=(s_ve if dc == 7 else None))

    if do_l0:
        jobs = ([("fc1", g, hh, 2048) for g in range(8) for hh in range(2)]
                + [("fc2", g, 0, 1024) for g in range(32)])
        sc0, mm0, ve0 = s_sc.n, s_mm.n, s_ve.n
        qmark = {}
        stmark = {}
        for j, (kind, g, hh, w) in enumerate(jobs):
            slot = j % 2
            wq = [] if j < 2 else [(s_sc, sc0 + (j - 1))]
            src = (fc1q_h[g * 128:(g + 1) * 128, hh * 2048:(hh + 1) * 2048]
                   if kind == "fc1" else fc2q_h[g * 128:(g + 1) * 128, :])
            qmark[j] = s_ld.n + 16
            pr.op("sync", "dma_start", inc=s_ld, inck=16, waits=wq,
                  out=qint[:, slot * 2048: slot * 2048 + w], in_=src)
            wr = [(s_mm, mm0)] if j == 0 else [(s_mm, mm0 + j)]
            srcs = (fc1s_h[g:g + 1, hh * 2048:(hh + 1) * 2048] if kind == "fc1"
                    else fc2s_h[g:g + 1, :])
            rowmark = ("gp", s_gp.n + 16)
            pr.op("gpsimd", "dma_start", inc=s_gp, inck=16, waits=wr,
                  out=rowsB[64:65, 0:w], in_=srcs)
            wc = [(s_ld, qmark[j])]
            if j >= 2:
                wc.append((s_ve, ve0 + (j - 1)))
            pr.op("scalar", "activation", out=qbf[:, slot * 2048: slot * 2048 + w],
                  in_=qint[:, slot * 2048: slot * 2048 + w], func=AF.Copy,
                  bias=-128.0, waits=wc, inc=s_sc)
            wt = [(s_gp, rowmark[1])]
            if j > 0:
                wt.append((s_ve, ve0 + j))
            else:
                wt.append((s_ve, ve0))
                wt.append((s_sc, sc0))
            repl = onesr3[64:65, :] if kind == "fc1" else halfr[64:65, :]
            for b in range(w // 512):
                pr.op("tensor", "matmul", OB(b), repl,
                      rowsB[64:65, b * 512:(b + 1) * 512], start=True, stop=True,
                      waits=(wt if b == 0 else []),
                      inc=(s_mm if b == w // 512 - 1 else None))
            wm = [(s_sc, sc0 + j + 1), (s_mm, mm0 + j + 1)]
            if j >= 2:
                wm.append((s_ld, stmark[j - 2]))
            pr.op("vector", "tensor_tensor",
                  out=wtmp[:, slot * 2048: slot * 2048 + w],
                  in0=qbf[:, slot * 2048: slot * 2048 + w],
                  in1=O_ps[:, 0:w], op=ALU.mult, waits=wm, inc=s_ve)
            if kind == "fc1":
                dst = w1s[:, hh * 8:(hh + 1) * 8, g * 256:(g + 1) * 256]
                src_ = (wtmp[:, slot * 2048: slot * 2048 + w]
                        .rearrange("p (cc j) -> p cc j", cc=8))
            else:
                dst = w2s[g * 128:(g + 1) * 128, :]
                src_ = wtmp[:, slot * 2048: slot * 2048 + w]
            stmark[j] = s_ld.n + 16
            pr.op("sync", "dma_start", inc=s_ld, inck=16,
                  waits=[(s_ve, ve0 + j + 1)],
                  out=dst, in_=src_)

    barrier()
    if do_l0:
        tap_dram("w1sd", w1s)
        tap_dram("w2sd", w2s)

    stkS.close()
    sbM = lambda n, sh, dt: stk.enter_context(nc.sbuf_tensor(n, sh, dt))
    xn_sb = sbM("xn_sb", [128, NC8 * T], BF16)
    yo_sb = sbM("yo_sb", [128, NC8 * T], BF16)
    ring = sbM("ring", [128, 3 * 2048], BF16)

    lall = rowsB[0:16, :]
    lrow = rowsB[64:65, :]
    RA = lambda r, sl: rowsA[r:r + 1, sl]
    RB = lambda r, sl: rowsB[r:r + 1, sl]
    xch = lambda c, t0=0, w=T: x_sb[:, c * T + t0: c * T + t0 + w]
    xnch = lambda c, t0=0, w=T: xn_sb[:, c * T + t0: c * T + t0 + w]
    SL = slice(0, T)

    def emit_stats(sqstage):
        sc0 = s_sc.n
        ve0 = s_ve.n
        mm0 = s_mm.n
        for c in range(NC8):
            wcast = [] if c == 0 else [(s_mm, mm0 + 2 * c)]
            pr.op("vector", "tensor_copy", out=pexp_sb[:], in_=xch(c),
                  waits=wcast, inc=s_ve)
            pr.op("scalar", "activation", out=sqstage[:], in_=xch(c),
                  func=AF.Square, waits=wcast, inc=s_sc)
            pr.wait("tensor", s_ve, ve0 + c + 1)
            for b in range(4):
                pr.op("tensor", "matmul", S_ps[0:1, b * 512:(b + 1) * 512],
                      onesc_bf[:], pexp_sb[:, b * 512:(b + 1) * 512],
                      start=(c == 0), stop=(c == NC8 - 1),
                      inc=(s_mm if b == 3 else None))
            pr.wait("tensor", s_sc, sc0 + c + 1)
            for b in range(4):
                pr.op("tensor", "matmul", O_ps[0:1, b * 512:(b + 1) * 512],
                      onesc_bf[:], sqstage[:, b * 512:(b + 1) * 512],
                      start=(c == 0), stop=(c == NC8 - 1),
                      inc=(s_mm if b == 3 else None))
        pr.wait("vector", s_mm)
        pr.op("vector", "tensor_scalar", out=RA(0, SL), in0=S_ps[0:1, :],
              scalar1=1.0 / D, scalar2=None, op0=ALU.mult)
        pr.op("vector", "tensor_scalar", out=RB(64, SL), in0=O_ps[0:1, :],
              scalar1=1.0 / D, scalar2=None, op0=ALU.mult, inc=s_ve)
        pr.wait("vector", s_ve)
        pr.op("vector", "tensor_tensor", out=RA(64, SL), in0=RA(0, SL),
              in1=RA(0, SL), op=ALU.mult, inc=s_ve)
        pr.wait("vector", s_ve)
        pr.op("vector", "tensor_tensor", out=RA(64, SL), in0=RB(64, SL),
              in1=RA(64, SL), op=ALU.subtract, inc=s_ve)
        pr.wait("scalar", s_ve)
        pr.op("scalar", "activation", out=RB(0, SL), in_=RA(64, SL), func=AF.Ln,
              bias=epsc[0:1, 0:1])
        pr.op("scalar", "activation", out=RA(32, SL), in_=RB(0, SL), func=AF.Exp,
              scale=-0.5, inc=s_sc)

    def emit_L0(s, qp_sb, kp_sb, vp_sb, h2_sb):
        if s > 0:
            pr.op("sync", "dma_start", inc=s_ld, inck=16,
                  waits=[(s_st, s_st.n)],
                  out=x_sb[:].rearrange("p (c t) -> p c t", c=NC8),
                  in_=x_h[s].rearrange("(c p) t -> p c t", p=128))
        pr.wait("scalar", s_ld)
        pr.wait("tensor", s_ld)
        pr.wait("vector", s_ld)
        if lvl < 1:
            return
        emit_stats(h2_sb[:, 0:T])
        pr.op("scalar", "activation", out=pexp_sb[0:1, :], in_=RA(64, SL),
              func=AF.Ln, scale=1.0 + EPS, bias=epsc[0:1, 1:2], inc=s_sc)
        pr.wait("vector", s_sc)
        pr.op("vector", "tensor_tensor", out=RB(0, SL), in0=RB(0, SL),
              in1=pexp_sb[0:1, :], op=ALU.subtract, inc=s_ve)
        pr.wait("scalar", s_ve)
        pr.op("scalar", "activation", out=RB(0, SL), in_=RB(0, SL), func=AF.Exp,
              scale=0.5, inc=s_sc)
        barrier()

        if lvl < 2:
            return
        prev_zbase = None
        prev_fc2_end = None
        prev_gadd = None
        for tt in range(NT):
            t0 = tt * TW
            tsl = slice(t0, t0 + TW)
            wr0 = [] if prev_zbase is None else [(s_ve, prev_zbase + 1)]
            wr1 = [] if prev_zbase is None else [(s_ve, prev_zbase + 2)]
            wr2 = [] if prev_zbase is None else [(s_ve, prev_zbase + 3)]
            pr.op("tensor", "matmul", SB(0), onesr_f[:], RA(0, tsl),
                  start=True, stop=True, waits=wr0)
            pr.op("tensor", "matmul", SB(1), onesr3[32:33, 0:128], RA(32, tsl),
                  start=True, stop=True, waits=wr1)
            pr.op("tensor", "matmul", SB(2), onesr_f[:], RB(0, tsl),
                  start=True, stop=True, waits=wr2, inc=s_mm)
            pr.wait("vector", s_mm)
            if prev_gadd is not None:
                pr.wait("vector", s_gp, prev_gadd)
            ve_h2base = s_ve.n
            for c in range(NC8):
                pr.op("vector", "tensor_tensor", out=stg[:], in0=xch(c, t0, TW),
                      in1=SB(0), op=ALU.subtract)
                pr.op("vector", "tensor_tensor", out=stg2[:], in0=stg[:],
                      in1=SB(1), op=ALU.mult)
                pr.op("vector", "tensor_copy", out=xnch(c, t0, TW), in_=stg2[:])
                pr.op("vector", "tensor_tensor", out=xch(c, t0, TW),
                      in0=xch(c, t0, TW), in1=stg2[:], op=ALU.add)
                pr.op("vector", "tensor_tensor", out=stg[:], in0=xnch(c, t0, TW),
                      in1=SB(2), op=ALU.mult)
                pr.op("vector", "tensor_scalar",
                      out=h2_sb[:, c * TW:(c + 1) * TW], in0=stg[:],
                      scalar1=cols[:, COL["gamma"] + c:COL["gamma"] + c + 1],
                      scalar2=cols[:, COL["beta"] + c:COL["beta"] + c + 1],
                      op0=ALU.mult, op1=ALU.add, inc=s_ve)
            ve_h2 = s_ve.n
            if tt == 1:
                tap_store("h2t0", h2_sb[:])
            ld0 = s_ld.n
            sc0f = s_sc.n

            def w1slot(pld):
                return (ring, (pld % 4) * 2048) if pld % 4 < 3 else (pexp_sb, 0)
            for pld in range(16):
                if pld < 4:
                    w = [] if prev_fc2_end is None else [(s_mm, prev_fc2_end)]
                    if pld == 3 and tt == 0:
                        w += [(s_mm, s_mm.n), (s_ve, s_ve.n), (s_sc, s_sc.n),
                              (s_l, s_l.n), (s_gp, s_gp.n)]
                else:
                    w = [(s_sc, sc0f + 2 * (pld - 3))]
                st_, so_ = w1slot(pld)
                pr.op("sync", "dma_start", inc=s_ld, inck=16, waits=w,
                      out=st_[:, so_:so_ + 2048]
                      .rearrange("p (o c) -> p o c", o=1),
                      in_=w1s[:, pld:pld + 1, :])
            for fc in range(32):
                pld = fc // 2
                st_, so_ = w1slot(pld)
                slot = so_ + (fc % 2) * 128
                bank = SB(3) if fc % 2 == 0 else OB(0)
                wts_ = [(s_ld, ld0 + 16 * (pld + 1))]
                if fc >= 2:
                    wts_.append((s_sc, sc0f + fc - 1))
                elif prev_zbase is not None:
                    wts_.append((s_ve, prev_zbase + (4 if fc == 0 else 5)))
                for hc in range(NC8):
                    w2_ = list(wts_) if hc == 0 else []
                    if fc < 2:
                        w2_.append((s_ve, ve_h2base + hc + 1))
                    pr.op("tensor", "matmul", bank,
                          st_[:, slot + hc * 256: slot + hc * 256 + 128],
                          h2_sb[:, hc * TW:(hc + 1) * TW],
                          start=(hc == 0), stop=(hc == 7),
                          waits=w2_,
                          inc=(s_mm if hc == 7 else None))
                pr.wait("scalar", s_mm)
                pr.op("scalar", "activation",
                      out=yo_sb[:, fc * TW:(fc + 1) * TW], in_=bank,
                      func=AF.Silu,
                      bias=cols[:, COL["fc1b"] + fc:COL["fc1b"] + fc + 1],
                      inc=s_sc)
            if tt == 0:
                tap_store("yt0", yo_sb[:])
            ld1 = s_ld.n
            mm1 = s_mm.n
            sc1 = s_sc.n

            def w2slot(pld):
                return (ring, (pld % 4) * 2048) if pld % 4 < 3 else (h2_sb, 0)
            for pld in range(16):
                w = [(s_sc, sc1)] if pld < 4 else [(s_mm, mm1 + 2 * (pld - 3))]
                st_, so_ = w2slot(pld)
                pr.op("sync", "dma_start", inc=s_ld, inck=16, waits=w,
                      out=st_[:, so_:so_ + 2048]
                      .rearrange("p (q h) -> p q h", q=2),
                      in_=w2s[pld * 256:(pld + 1) * 256, :]
                      .rearrange("(q p) h -> p q h", p=128))
            for fc in range(32):
                pld = fc // 2
                st_, so_ = w2slot(pld)
                slot = so_ + (fc % 2) * 1024
                w = [(s_ld, ld1 + 16 * (pld + 1))]
                if fc == 0:
                    w.append((s_sc, sc1))
                for hc in range(NC8):
                    bank = SB(hc) if hc < 4 else OB(hc - 4)
                    pr.op("tensor", "matmul", bank,
                          st_[:, slot + hc * 128: slot + hc * 128 + 128],
                          yo_sb[:, fc * TW:(fc + 1) * TW],
                          start=(fc == 0), stop=(fc == 31),
                          waits=(w if hc == 0 else []),
                          inc=(s_mm if hc == 7 else None))
            prev_fc2_end = s_mm.n
            zb = s_ve.n
            gp_add0 = s_gp.n
            for hc in range(NC8):
                bank = SB(hc) if hc < 4 else OB(hc - 4)
                sslot = stg2 if hc % 2 == 0 else stg
                wv_ = [(s_mm, s_mm.n)]
                if hc >= 2:
                    wv_.append((s_gp, gp_add0 + hc - 1))
                pr.op("vector", "tensor_scalar", out=sslot[:], in0=bank,
                      scalar1=cols[:, COL["fc2b5"] + hc:COL["fc2b5"] + hc + 1],
                      scalar2=None, op0=ALU.add, waits=wv_, inc=s_ve)
                pr.op("gpsimd", "tensor_tensor", out=xch(hc, t0, TW),
                      in0=xch(hc, t0, TW), in1=sslot[:], op=ALU.add,
                      waits=[(s_ve, s_ve.n)], inc=s_gp)
            prev_zbase = zb
            prev_gadd = s_gp.n
        barrier()
        tap_store("xn", xn_sb[:])
        tap_store("cffn", x_sb[:])

        if lvl < 3:
            return
        pr.op("vector", "memset", vp_sb[:], 0.0, inc=s_ve)
        import os
        NHP = int(os.environ.get("K_NHP", "8"))
        NHI = int(os.environ.get("K_NHI", "2"))
        NKT = int(os.environ.get("K_NKT", "16"))
        KSUB = int(os.environ.get("K_SUB", "99"))
        gpmark = {}
        mm_pair_end = {}

        def qkv_loads(hp):
            pi = hp % 2
            gpmark[hp] = s_gp.n
            if hp < 2:
                w = [(s_mm, s_mm.n)]
            else:
                w = [(s_mm, mm_pair_end[hp - 2])]
            for j, colof in enumerate([hp * 128, D + hp * 128,
                                       2 * D + hp * 128]):
                pr.op("gpsimd", "dma_start", inc=s_gp, inck=16,
                      waits=(w if j == 0 else []),
                      out=ring[:, pi * 3072 + j * 1024: pi * 3072 + j * 1024 + 1024]
                      .rearrange("p (c j2) -> p c j2", c=8),
                      in_=wqkvT_h[:, colof:colof + 128]
                      .rearrange("(c p) j2 -> p c j2", p=128))
        qkv_loads(0)
        if NHP > 1:
            qkv_loads(1)
        for hp in range(NHP):
            pi = hp % 2
            rb = pi * 3072
            if KSUB < 2:
                continue
            pr.wait("tensor", s_gp, gpmark[hp] + 16)
            pr.wait("tensor", s_sc)
            for tt in range(NT):
                for c in range(NC8):
                    pr.op("tensor", "matmul", S_ps[:, tt * 512:(tt + 1) * 512],
                          ring[:, rb + c * 128: rb + (c + 1) * 128],
                          xnch(c, tt * TW, TW), start=(c == 0), stop=(c == 7),
                          inc=(s_mm if (tt == 3 and c == 7) else None))
            pr.wait("tensor", s_gp, gpmark[hp] + 32)
            for tt in range(NT):
                for c in range(NC8):
                    pr.op("tensor", "matmul", O_ps[:, tt * 512:(tt + 1) * 512],
                          ring[:, rb + 1024 + c * 128: rb + 1024 + c * 128 + 128],
                          xnch(c, tt * TW, TW), start=(c == 0), stop=(c == 7),
                          inc=(s_mm if (tt == 3 and c == 7) else None))
            pr.wait("scalar", s_mm, s_mm.n - 1)
            pr.op("scalar", "copy", out=qp_sb[:], in_=S_ps[:], inc=s_sc)
            pr.wait("scalar", s_mm)
            pr.op("scalar", "copy", out=kp_sb[:], in_=O_ps[:], inc=s_sc)
            if KSUB < 3:
                continue
            for j, src_sb in enumerate([qp_sb, kp_sb]):
                pr.wait("scalar", s_sc, s_sc.n - 1 + j)
                if j == 1:
                    pr.wait("scalar", s_ve)
                pr.op("scalar", "activation", out=pexp_sb[:], in_=src_sb[:],
                      func=AF.Square, inc=s_sc)
                pr.wait("tensor", s_sc)
                for b in range(4):
                    pr.op("tensor", "matmul", O_ps[0:1, b * 512:(b + 1) * 512],
                          onesc_bf[:], pexp_sb[:, b * 512:(b + 1) * 512],
                          start=True, stop=True,
                          inc=(s_mm if b == 3 else None))
                pr.wait("vector", s_mm)
                pr.op("vector", "tensor_reduce", out=sca[0:1, j:j + 1],
                      in_=O_ps[0:1, :], axis=AXX, op=ALU.max, inc=s_ve)
            pr.wait("vector", s_ve)
            pr.op("vector", "tensor_tensor", out=sca[0:1, 2:3], in0=sca[0:1, 0:1],
                  in1=sca[0:1, 1:2], op=ALU.add, inc=s_ve)
            pr.wait("vector", s_ve)
            pr.op("vector", "tensor_scalar", out=sca[0:1, 2:3], in0=sca[0:1, 2:3],
                  scalar1=1.0 / 16, scalar2=None, op0=ALU.mult, inc=s_ve)
            pr.wait("vector", s_ve)
            pr.op("vector", "tensor_scalar", out=sca[0:1, 3:4], in0=sca[0:1, 2:3],
                  scalar1=-1.0, scalar2=None, op0=ALU.mult, inc=s_ve)
            pr.wait("scalar", s_ve)
            pr.op("scalar", "activation", out=sca[0:1, 4:5], in_=sca[0:1, 2:3],
                  func=AF.Exp, inc=s_sc)
            if KSUB < 4:
                continue
            pr.wait("tensor", s_sc)
            cc = hp * 2
            pr.op("tensor", "matmul", O_ps[0:128, cc:cc + 1], onesr_f[:],
                  sca[0:1, 3:4], start=True, stop=True)
            pr.op("tensor", "matmul", O_ps[0:128, cc + 1:cc + 2], onesr_f[:],
                  sca[0:1, 4:5], start=True, stop=True, inc=s_mm)
            pr.wait("vector", s_mm)
            pr.op("vector", "tensor_copy", out=c0neg[:],
                  in_=O_ps[0:128, cc:cc + 1])
            pr.op("vector", "tensor_copy", out=ecol[:],
                  in_=O_ps[0:128, cc + 1:cc + 2], inc=s_ve)
            pr.wait("vector", s_ve)

            if KSUB < 5:
                continue
            vev0 = s_ve.n
            for Q in range(4):
                bank = SB(0) if Q % 2 == 0 else SB(1)
                w = [(s_ve, vev0)] if Q < 2 else [(s_ve, vev0 + (Q - 1) * 4)]
                for c in range(NC8):
                    for i in range(4):
                        tc = 4 * Q + i
                        pr.op("tensor", "matmul",
                              bank[0:128, i * 128:(i + 1) * 128],
                              xn_sb[:, c * T + tc * 128: c * T + (tc + 1) * 128],
                              ring[:, rb + 2048 + c * 128: rb + 2048 + c * 128 + 128],
                              start=(c == 0), stop=(c == 7),
                              waits=(w if (c == 0 and i == 0) else []),
                              inc=(s_mm if (c == 7 and i == 3) else None))
                pr.wait("vector", s_mm)
                for i in range(4):
                    tc = 4 * Q + i
                    base = tc * 144
                    pr.op("vector", "tensor_scalar",
                          out=vp_sb[:, base:base + 64],
                          in0=bank[0:128, i * 128:i * 128 + 64],
                          scalar1=ecol[:], scalar2=None, op0=ALU.mult)
                    pr.op("vector", "tensor_scalar",
                          out=vp_sb[:, base + 72:base + 136],
                          in0=bank[0:128, i * 128 + 64:i * 128 + 128],
                          scalar1=ecol[:], scalar2=None, op0=ALU.mult)
                    pr.op("vector", "tensor_scalar",
                          out=vp_sb[:, base + 64:base + 65], in0=onesc_bf[:],
                          scalar1=ecol[:], scalar2=None, op0=ALU.mult)
                    pr.op("vector", "tensor_scalar",
                          out=vp_sb[:, base + 136:base + 137], in0=onesc_bf[:],
                          scalar1=ecol[:], scalar2=None, op0=ALU.mult,
                          inc=s_ve)
            mm_pair_end[hp] = s_mm.n
            if hp + 2 < NHP:
                qkv_loads(hp + 2)

            prev_exp_last = None
            for hi in range(NHI):
                qs = slice(64 * hi, 64 * hi + 64)
                h = 2 * hp + hi
                pbuf = [pexp_sb, h2_sb]
                exp_mark = {}
                av_mark = {}
                mask_mark = {}
                ve_ent = s_ve.n
                sl_ent = s_l.n
                sc_ent = s_sc.n
                for kt in range(NKT):
                    qd = kt // 4
                    pb = pbuf[kt % 2]
                    for qtile in range(qd, 4):
                        wv = []
                        if (kt - 1, qtile) in exp_mark:
                            wv.append((s_sc, exp_mark[(kt - 1, qtile)]))
                        elif kt == 0:
                            if prev_exp_last is not None:
                                wv.append((s_sc, prev_exp_last[qtile]))
                            else:
                                wv.append((s_sc, sc_ent))
                            if hi == 0:
                                wv.append((s_ve, ve_ent))
                        pr.op("tensor", "matmul",
                              S_ps[:, qtile * 512:(qtile + 1) * 512],
                              kp_sb[qs, kt * 128:(kt + 1) * 128],
                              qp_sb[qs, qtile * 512:(qtile + 1) * 512],
                              start=True, stop=True, waits=wv, inc=s_mm)
                        mmv = s_mm.n
                        wexp = [(s_mm, mmv)]
                        if (kt - 2, qtile) in av_mark:
                            wexp.append((s_mm, av_mark[(kt - 2, qtile)]))
                        if kt == 0:
                            wexp += [(s_ve, ve_ent), (s_l, sl_ent)]
                        pr.op("scalar", "activation",
                              out=pb[:, qtile * 512:(qtile + 1) * 512],
                              in_=S_ps[:, qtile * 512:(qtile + 1) * 512],
                              func=AF.Exp, scale=0.125,
                              bias=c0neg[:], waits=wexp, inc=s_sc)
                        exp_mark[(kt, qtile)] = s_sc.n
                    r = kt * 128 - qd * 512
                    pr.op("vector", "tensor_tensor",
                          out=pb[:, qd * 512:qd * 512 + 512],
                          in0=pb[:, qd * 512:qd * 512 + 512],
                          in1=band_t[:, 384 - r:384 - r + 512], op=ALU.mult,
                          waits=[(s_sc, exp_mark[(kt, qd)])], inc=s_ve)
                    mask_mark[kt] = s_ve.n
                    for qtile in range(qd, 4):
                        if qtile == qd:
                            wav = [(s_ve, mask_mark[kt])]
                        else:
                            wav = [(s_sc, exp_mark[(kt, qtile)])]
                        pr.op("tensor", "matmul",
                              O_ps[0:65, qtile * 512:(qtile + 1) * 512],
                              vp_sb[:, kt * 144 + 72 * hi:
                                    kt * 144 + 72 * hi + 65],
                              pb[:, qtile * 512:(qtile + 1) * 512],
                              start=(kt == 0), stop=(kt == 4 * qtile + 3),
                              waits=wav, inc=s_mm)
                        av_mark[(kt, qtile)] = s_mm.n
                if True:
                    prev_exp_last = {q: exp_mark[(4 * q + 3, q)] for q in range(4)}
                pr.wait("scalar", s_mm)
                pr.op("scalar", "copy", out=pexp_sb[0:65, :], in_=O_ps[0:65, :],
                      inc=s_sc)
                pr.wait("gpsimd", s_sc)
                pr.op("gpsimd", "dma_start", inc=s_l, inck=16,
                      out=rowsB[h:h + 1, :], in_=pexp_sb[64:65, :])
                pr.wait("vector", s_sc)
                for tt in range(NT):
                    pr.op("vector", "tensor_copy",
                          out=yo_sb[64 * hi:64 * hi + 64,
                                    hp * T + tt * 512: hp * T + tt * 512 + 512],
                          in_=pexp_sb[0:64, tt * 512:(tt + 1) * 512],
                          inc=(s_ve if tt == 3 else None))
            barrier()
        tap_store("c0neg", c0neg[:])
        tap_store("ecoltap", ecol[:])
        tap_store("qp7", qp_sb[:])
        tap_store("kp7", kp_sb[:])
        tap_store("vp7", vp_sb[:])
        tap_store("ouns", yo_sb[:])
        tap_store("lall", rowsB[0:16, :])

        if lvl < 4:
            return
        pr.wait("vector", s_l)
        pr.op("vector", "reciprocal", out=rowsB[0:16, :], in_=rowsB[0:16, :], inc=s_ve)
        pr.wait("scalar", s_ve)
        pr.op("scalar", "copy", out=pexp_sb[0:16, :], in_=rowsB[0:16, :], inc=s_sc)
        lstage = [pexp_sb[64:65, :], h2_sb[64:65, 0:T]]
        sc_cast = s_sc.n
        mm_base_l = s_mm.n
        gp_l0 = s_gp.n
        pr.op("gpsimd", "dma_start", inc=s_gp, inck=16,
              waits=[(s_sc, sc_cast), (s_mm, mm_base_l)],
              out=lstage[0], in_=pexp_sb[0:1, :])
        pr.op("gpsimd", "dma_start", inc=s_gp, inck=16,
              out=lstage[1], in_=pexp_sb[1:2, :])
        mm_lrep = {}
        for h in range(H):
            hp, hi = h // 2, h % 2
            pr.wait("tensor", s_gp, gp_l0 + 16 * (h + 1))
            pr.wait("tensor", s_ve)
            st_l = lstage[h % 2]
            for tt in range(NT):
                pr.op("tensor", "matmul", SB(tt)[0:64, :], onesr_b3[64:65, 0:64],
                      st_l[0:1, tt * 512:(tt + 1) * 512], start=True, stop=True,
                      inc=(s_mm if tt == 3 else None))
            mm_lrep[h] = s_mm.n
            if h + 2 < H:
                pr.op("gpsimd", "dma_start", inc=s_gp, inck=16,
                      waits=[(s_mm, mm_lrep[h])],
                      out=lstage[h % 2], in_=pexp_sb[h + 2:h + 3, :])
            pr.wait("vector", s_mm)
            for tt in range(NT):
                dst = yo_sb[64 * hi:64 * hi + 64,
                            hp * T + tt * 512: hp * T + tt * 512 + 512]
                pr.op("vector", "tensor_tensor", out=dst, in0=dst,
                      in1=SB(tt)[0:64, :], op=ALU.mult,
                      inc=(s_ve if tt == 3 else None))
        barrier()
        tap_store("onrm", yo_sb[:])

        if lvl < 5:
            return
        for o2c in range(8):
            gpw = [] if o2c < 2 else [(s_ve, s_ve.n)]
            gpld = s_gp.n
            pr.op("gpsimd", "dma_start", inc=s_gp, inck=16, waits=gpw,
                  out=ring[:, (o2c % 2) * 2048: (o2c % 2) * 2048 + 1024]
                  .rearrange("p (c j) -> p c j", c=8),
                  in_=woutT_h[:, o2c * 128:(o2c + 1) * 128]
                  .rearrange("(c p) j -> p c j", p=128))
            pr.wait("tensor", s_gp, gpld + 16)
            for tt in range(NT):
                bank = OB(tt % 2)
                wbk = [(s_ve, s_ve.n)] if (tt >= 2 or o2c > 0) else []
                for c in range(NC8):
                    pr.op("tensor", "matmul", bank,
                          ring[:, (o2c % 2) * 2048 + c * 128:
                               (o2c % 2) * 2048 + c * 128 + 128],
                          yo_sb[:, c * T + tt * 512: c * T + tt * 512 + 512],
                          start=(c == 0), stop=(c == 7),
                          waits=(wbk if c == 0 else []),
                          inc=(s_mm if c == 7 else None))
                sslot = stg if tt % 2 == 0 else stg2
                wv_ = [(s_mm, s_mm.n)]
                if tt >= 2 or o2c > 0:
                    wv_.append((s_gp, s_gp.n - 1))
                pr.op("vector", "tensor_scalar", out=sslot[:], in0=bank,
                      scalar1=cols[:, COL["bout"] + o2c:COL["bout"] + o2c + 1],
                      scalar2=None, op0=ALU.add, waits=wv_, inc=s_ve)
                pr.op("gpsimd", "tensor_tensor", out=xch(o2c, tt * TW, TW),
                      in0=xch(o2c, tt * TW, TW), in1=sslot[:], op=ALU.add,
                      waits=[(s_ve, s_ve.n)], inc=s_gp)
        barrier()
        tap_store("x1", x_sb[:])

    def emit_L1(s, scanA, scanB, sqbf):
        if lvl < 6:
            return
        emit_stats(sqbf)
        barrier()
        for tt in range(NT):
            t0 = tt * TW
            tsl = slice(t0, t0 + TW)
            pr.op("tensor", "matmul", SB(0), onesr_f[:], RA(0, tsl),
                  start=True, stop=True)
            pr.op("tensor", "matmul", SB(1), onesr3[32:33, 0:128], RA(32, tsl),
                  start=True, stop=True, inc=s_mm)
            pr.wait("vector", s_mm)
            for c in range(NC8):
                pr.op("vector", "tensor_tensor", out=stg[:], in0=xch(c, t0, TW),
                      in1=SB(0), op=ALU.subtract)
                pr.op("vector", "tensor_tensor", out=xnch(c, t0, TW), in0=stg[:],
                      in1=SB(1), op=ALU.mult,
                      inc=(s_ve if c == NC8 - 1 else None))
            pr.wait("tensor", s_ve)
        barrier()
        tap_store("xn1", xn_sb[:])

        if lvl < 7:
            return
        pr.op("gpsimd", "dma_start", inc=s_gp, inck=16,
              out=ring[:, 4096:4480].rearrange("p (c j) -> p c j", c=8),
              in_=win_h[:, 2 * D: 2 * D + 48].rearrange("(c p) j -> p c j", p=128))
        pr.wait("tensor", s_gp)
        sdst = [scanA[0:16, :], scanB[0:16, :], rowsA[0:16, :]]
        for j, dst in enumerate(sdst):
            wbk = [(s_ve, s_ve.n)] if j > 0 else []
            for tt in range(NT):
                for c in range(NC8):
                    pr.op("tensor", "matmul",
                          S_ps[0:16, tt * 512:(tt + 1) * 512],
                          ring[:, 4096 + c * 48 + 16 * j: 4096 + c * 48 + 16 * j + 16],
                          xnch(c, tt * TW, TW), start=(c == 0), stop=(c == 7),
                          waits=(wbk if (tt == 0 and c == 0) else []),
                          inc=(s_mm if (tt == 3 and c == 7) else None))
            pr.wait("vector", s_mm)
            pr.op("vector", "tensor_scalar", out=dst,
                  in0=S_ps[0:16, :], scalar1=sc16[0:16, j:j + 1],
                  scalar2=None, op0=ALU.add, inc=s_ve)
        bcd_ve = s_ve.n
        dl = rowsA[0:16, :]
        tmp = rowsB[0:16, :]
        pr.op("scalar", "activation", out=tmp, in_=dl, func=AF.Abs,
              waits=[(s_ve, bcd_ve)])
        pr.op("scalar", "activation", out=tmp, in_=tmp, func=AF.Exp, scale=-1.0)
        pr.op("scalar", "activation", out=tmp, in_=tmp, func=AF.Ln, bias=1.0,
              inc=s_sc)
        pr.wait("vector", s_sc)
        pr.op("vector", "tensor_scalar", out=dl, in0=dl,
              scalar1=0.0, scalar2=None, op0=ALU.max, inc=s_ve)
        pr.wait("vector", s_ve)
        pr.op("vector", "tensor_tensor", out=dl, in0=dl, in1=tmp, op=ALU.add,
              inc=s_ve)
        pr.wait("vector", s_ve)
        pr.op("vector", "tensor_scalar", out=tmp, in0=dl,
              scalar1=sc16[0:16, 4:5], scalar2=None, op0=ALU.mult, inc=s_ve)
        pr.wait("scalar", s_ve)
        pr.op("scalar", "activation", out=tmp, in_=tmp, func=AF.Exp, inc=s_sc)
        pr.wait("vector", s_sc)
        pr.op("vector", "tensor_tensor_scan", out=dl,
              data0=tmp, data1=scanA[0:16, :], initial=0.0,
              op0=ALU.mult, op1=ALU.add, inc=s_ve)
        pr.wait("vector", s_ve)
        pr.op("vector", "tensor_tensor", out=scanB[0:16, :],
              in0=scanB[0:16, :], in1=dl, op=ALU.mult, inc=s_ve)
        scan_ve = s_ve.n

        for pt in range(8):
            gpw = [] if pt < 2 else [(s_sc, s_sc.n)]
            gpld = s_gp.n
            pr.op("gpsimd", "dma_start", inc=s_gp, inck=16, waits=gpw,
                  out=ring[:, (pt % 2) * 2048:(pt % 2) * 2048 + 1024]
                  .rearrange("p (c j) -> p c j", c=8),
                  in_=win_h[:, D + pt * 128: D + (pt + 1) * 128]
                  .rearrange("(c p) j -> p c j", p=128))
            pr.wait("tensor", s_gp, gpld + 16)
            for tt in range(NT):
                bank = OB(tt % 2)
                wbk = [(s_sc, s_sc.n)] if tt >= 2 or pt > 0 else []
                for c in range(NC8):
                    pr.op("tensor", "matmul", bank,
                          ring[:, (pt % 2) * 2048 + c * 128:
                               (pt % 2) * 2048 + c * 128 + 128],
                          xnch(c, tt * TW, TW), start=(c == 0), stop=(c == 7),
                          waits=(wbk if c == 0 else []),
                          inc=(s_mm if c == 7 else None))
                pr.wait("scalar", s_mm)
                pr.op("scalar", "activation",
                      out=yo_sb[:, pt * T + tt * 512: pt * T + tt * 512 + 512],
                      in_=bank, func=AF.Gelu,
                      bias=cols[:, COL["binz"] + pt:COL["binz"] + pt + 1],
                      inc=s_sc)
        gate_sc = s_sc.n
        pr.wait("tensor", s_ve, scan_ve)
        for b in range(4):
            pr.op("tensor", "matmul", S_ps[0:1, b * 512:(b + 1) * 512],
                  sc16[0:16, 5:6], scanB[0:16, b * 512:(b + 1) * 512],
                  start=True, stop=True, inc=(s_mm if b == 3 else None))
        pr.wait("vector", s_mm)
        pr.op("vector", "tensor_copy", out=RB(32, SL), in_=S_ps[0:1, :],
              inc=s_ve)
        tap_store("scanS", rowsA[0:16, :])
        tap_store("yrow", rowsB[32:33, :])

        if lvl < 9:
            return
        pr.wait("vector", s_sc, gate_sc)
        pr.wait("tensor", s_ve)
        tap_store("gtap", yo_sb[:])
        for tt in range(NT):
            pr.op("tensor", "matmul", SB(0), onesr3[32:33, 0:128],
                  RB(32, slice(tt * 512, tt * 512 + 512)), start=True,
                  stop=True, inc=s_mm)
            pr.wait("vector", s_mm)
            for c in range(NC8):
                dst = yo_sb[:, c * T + tt * 512: c * T + tt * 512 + 512]
                pr.op("vector", "tensor_tensor", out=dst, in0=dst, in1=SB(0),
                      op=ALU.mult, inc=(s_ve if c == NC8 - 1 else None))
            pr.wait("tensor", s_ve)
        barrier()
        tap_store("utap", yo_sb[:])

        if lvl < 10:
            return
        for o2c in range(8):
            gpw = [] if o2c < 2 else [(s_ve, s_ve.n)]
            gpld = s_gp.n
            pr.op("gpsimd", "dma_start", inc=s_gp, inck=16, waits=gpw,
                  out=ring[:, (o2c % 2) * 2048:(o2c % 2) * 2048 + 1024]
                  .rearrange("p (c j) -> p c j", c=8),
                  in_=wout1_h[:, o2c * 128:(o2c + 1) * 128]
                  .rearrange("(c p) j -> p c j", p=128))
            pr.wait("tensor", s_gp, gpld + 16)
            for tt in range(NT):
                bank = OB(tt % 2)
                wbk = [(s_ve, s_ve.n)] if (tt >= 2 or o2c > 0) else []
                for c in range(NC8):
                    pr.op("tensor", "matmul", bank,
                          ring[:, (o2c % 2) * 2048 + c * 128:
                               (o2c % 2) * 2048 + c * 128 + 128],
                          yo_sb[:, c * T + tt * 512: c * T + tt * 512 + 512],
                          start=(c == 0), stop=False,
                          waits=(wbk if c == 0 else []))
                for c in range(NC8):
                    pr.op("tensor", "matmul", bank,
                          wbar_sb[:, c * D + o2c * 128: c * D + (o2c + 1) * 128],
                          xnch(c, tt * TW, TW), start=False, stop=(c == 7),
                          inc=(s_mm if c == 7 else None))
                sslot = stg if tt % 2 == 0 else stg2
                wv_ = [(s_mm, s_mm.n)]
                if tt >= 2 or o2c > 0:
                    wv_.append((s_gp, s_gp.n - 1))
                pr.op("vector", "tensor_scalar", out=sslot[:], in0=bank,
                      scalar1=cols[:, COL["badd"] + o2c:COL["badd"] + o2c + 1],
                      scalar2=None, op0=ALU.add, waits=wv_, inc=s_ve)
                pr.op("gpsimd", "tensor_tensor", out=xch(o2c, tt * TW, TW),
                      in0=xch(o2c, tt * TW, TW), in1=sslot[:], op=ALU.add,
                      waits=[(s_ve, s_ve.n)], inc=s_gp)
        barrier()

    for s in range(NSEQ):
        if do_l0:
            stA = contextlib.ExitStack()
            qp_sb = stA.enter_context(nc.sbuf_tensor(f"qp{s}", [128, T], BF16))
            kp_sb = stA.enter_context(nc.sbuf_tensor(f"kp{s}", [128, T], BF16))
            vp_sb = stA.enter_context(nc.sbuf_tensor(f"vp{s}", [128, 2304], BF16))
            h2_sb = stA.enter_context(
                nc.sbuf_tensor(f"h2{s}", [128, NC8 * TW], BF16))
            emit_L0(s, qp_sb, kp_sb, vp_sb, h2_sb)
            stA.close()
        else:
            if s > 0:
                pr.op("sync", "dma_start", inc=s_ld, inck=16,
                      waits=[(s_st, s_st.n)],
                      out=x_sb[:].rearrange("p (c t) -> p c t", c=NC8),
                      in_=x_h[s].rearrange("(c p) t -> p c t", p=128))
            barrier()
        if do_l1:
            stB = contextlib.ExitStack()
            scanA = stB.enter_context(nc.sbuf_tensor(f"scA{s}", [128, T], F32))
            scanB = stB.enter_context(nc.sbuf_tensor(f"scB{s}", [128, T], F32))
            sqbf = stB.enter_context(nc.sbuf_tensor(f"sq{s}", [128, T], BF16))
            emit_L1(s, scanA, scanB, sqbf)
            stB.close()
        barrier()
        pr.op("sync", "dma_start", inc=s_st, inck=16,
              out=out_h[s].rearrange("(c p) t -> p c t", p=128),
              in_=x_sb[:].rearrange("p (c t) -> p c t", c=NC8))
    pr.wait("sync", s_st)

    with nc.Block() as block:
        for ename in Prog.ENGS:
            fns = pr.items[ename]
            if not fns:
                continue

            def body(eng, fns=fns):
                for fn in fns:
                    fn(eng)
            getattr(block, ename)(body)

    stk.close()
    return nc, tap_hs


def _mk_smallrow(inp):
    sr = np.zeros((1, SR_LEN), np.float32)
    vals = {"bout": inp["L0_attn_bout"], "gamma": inp["L0_ffn_ln_gamma"],
            "beta": inp["L0_ffn_ln_beta"], "fc1b": inp["L0_ffn_fc1_bias"],
            "fc2b": inp["L0_ffn_fc2_bias"],
            "binz": inp["L1_attn_b_in"][D:2 * D],
            "binB": inp["L1_attn_b_in"][2 * D:2 * D + 16],
            "binC": inp["L1_attn_b_in"][2 * D + 16:2 * D + 32],
            "binD": inp["L1_attn_b_in"][2 * D + 32:2 * D + 48],
            "alog": inp["L1_attn_A_log"], "bout1": inp["L1_attn_b_out"]}
    for name, v in vals.items():
        off, ln = _SR[name]
        sr[0, off:off + ln] = np.asarray(v, np.float32)
    return sr


def _consts():
    band = np.zeros((128, 896), np.float32)
    for i in range(128):
        band[i, i + 384:] = 1.0
    return {"i128f": np.eye(128, dtype=np.float32),
            "bandbf": band.astype(ml_dtypes.bfloat16)}


_FUSED = True
_TAPS = ()


def _host_inputs(inp):
    inp = {k: np.asarray(v) for k, v in inp.items()}
    x = np.ascontiguousarray(inp["L0_attn_x"].astype(np.float32))
    common = dict(smallrow=_mk_smallrow(inp), **_consts())
    l0w = dict(
        wqkvT=np.ascontiguousarray(inp["L0_attn_Wqkv"].T.astype(np.float32)),
        woutT=np.ascontiguousarray(inp["L0_attn_Wout"].T.astype(np.float32)),
        fc1q=np.ascontiguousarray(inp["L0_ffn_fc1_q"].astype(np.uint8)),
        fc2q=np.ascontiguousarray(inp["L0_ffn_fc2_q"].astype(np.uint8)),
        fc1s=np.ascontiguousarray(inp["L0_ffn_fc1_scales"].astype(np.float32)),
        fc2s=np.ascontiguousarray(inp["L0_ffn_fc2_scales"].astype(np.float32)))
    l1w = dict(
        win=np.ascontiguousarray(inp["L1_attn_W_in"].astype(np.float32)),
        wout1=np.ascontiguousarray(inp["L1_attn_W_out"].astype(np.float32)),
        l1w=np.ascontiguousarray(
            inp["L1_ffn_W"].astype(np.float32).reshape(32 * D, D)),
        l1bf=np.ascontiguousarray(
            inp["L1_ffn_b"].astype(np.float32).reshape(1, 32 * D)),
        ids=np.ascontiguousarray(inp["L1_ffn_ids"].astype(np.int32)),
        wts=np.ascontiguousarray(
            inp["L1_ffn_weights"].astype(np.float32).reshape(1, 8)))
    shards = [np.ascontiguousarray(x[2 * c:2 * c + 2].transpose(0, 2, 1))
              for c in range(NCORES)]
    return shards, common, l0w, l1w


def kernel(**inp):
    shards, common, l0w, l1w = _host_inputs(inp)
    if _FUSED:
        nc, _ = build(("L0", "L1"), taps=_TAPS)
        in_maps = [dict(xT=shards[c], **common, **l0w, **l1w)
                   for c in range(NCORES)]
        res = run_bass_kernel_spmd(nc, in_maps, core_ids=list(range(NCORES)))
        outs = [res.results[c]["outT"] for c in range(NCORES)]
    else:
        ncA, _ = build(("L0",), taps=_TAPS)
        in_maps = [dict(xT=shards[c], **common, **l0w) for c in range(NCORES)]
        resA = run_bass_kernel_spmd(ncA, in_maps, core_ids=list(range(NCORES)))
        x1s = [np.asarray(resA.results[c]["outT"]) for c in range(NCORES)]
        ncB, _ = build(("L1",), taps=_TAPS)
        in_mapsB = [dict(xT=np.ascontiguousarray(x1s[c]), **common, **l1w)
                    for c in range(NCORES)]
        resB = run_bass_kernel_spmd(ncB, in_mapsB, core_ids=list(range(NCORES)))
        outs = [resB.results[c]["outT"] for c in range(NCORES)]
    out = np.empty((16, T, D), np.float32)
    for c in range(NCORES):
        out[2 * c:2 * c + 2] = np.asarray(outs[c], np.float32).transpose(0, 2, 1)
    return out
